# version 04
# baseline (speedup 1.0000x reference)
import contextlib
import math
import numpy as np
import ml_dtypes

import concourse.bass as bass
import concourse.mybir as mybir
from concourse.bass_utils import run_bass_kernel_spmd

F32 = mybir.dt.float32
BF16 = mybir.dt.bfloat16
AF = mybir.ActivationFunctionType
ALU = mybir.AluOpType
ET = mybir.EngineType
AX = mybir.AxisListType

D = 1024
SEQ = 4096
NB = 4
NCORE = 8
TOK = 2048
NT = TOK // 512
KC = D // 128
BF = ml_dtypes.bfloat16
PI = float(np.pi)

_PROGRAMS = {}


import contextlib as _ctxlib


_FORI_CACHE = {}


@_ctxlib.contextmanager
def fori(nc, engines, start, end):
    from concourse.bass import OrderedEngineSet
    key = (id(nc), tuple(engines), start, end)
    eng_set = OrderedEngineSet(engines)
    loop_id = nc.next_id()
    ls, le = f"ff{loop_id}_l", f"ff{loop_id}_e"
    if key in _FORI_CACHE:
        regs, iv = _FORI_CACHE[key]
        nc.regs_mov(regs, start)
        nc.br(ls, engines=eng_set)
        with nc.body(ls, valid_engines=eng_set):
            nc.regs_mov(iv.val, regs)
            yield iv
            nc.regs_alu(regs, regs, 1, op=ALU.add)
            nc.br_lt(regs, end, on_true=ls, on_false=le, engines=eng_set)
        nc.switch_bb(le)
    else:
        regs = nc.alloc_registers(f"ff{loop_id}_i", engines=eng_set)
        nc.regs_mov(regs, start)
        nc.br(ls, engines=eng_set)
        with nc.body(ls, valid_engines=eng_set):
            iv = nc.snap(regs, min_val=start, max_val=end - 1)
            yield iv
            nc.regs_alu(regs, regs, 1, op=ALU.add)
            nc.br_lt(regs, end, on_true=ls, on_false=le, engines=eng_set)
        nc.switch_bb(le)
        _FORI_CACHE[key] = (regs, iv)


class Sem:
    def __init__(self, ctx, nc, name):
        self.h = ctx.enter_context(nc.semaphore(name))
        self.v = 0


class Q:
    def __init__(self, ctx, nc, eng, name, dma_step=1, selfsync=False):
        self.eng = eng
        self.sem = Sem(ctx, nc, "q_" + name)
        self.step = dma_step
        self.selfsync = selfsync

    def pre(self):
        if self.selfsync and self.sem.v:
            self.eng.wait_ge(self.sem.h, self.sem.v)

    def wait(self, other, val=None):
        sem = other.sem if isinstance(other, Q) else other
        v = sem.v if val is None else val
        self.eng.wait_ge(sem.h, v)

    def do(self, ins):
        ins.then_inc(self.sem.h, self.step)
        self.sem.v += self.step
        return ins

    def dma(self, out, in_, **kw):
        if self.step == 16:
            self.eng.wait_ge(self.sem.h, self.sem.v)
        return self.do(self.eng.dma_start(out=out, in_=in_, **kw))

    def mm(self, out, lhsT, rhs, start=True, stop=True, **kw):
        return self.do(self.eng.matmul(out, lhsT, rhs, start=start, stop=stop, **kw))

    def tr(self, out, in_, ident):
        self.pre()
        return self.do(self.eng.transpose(out, in_, ident))

    def tt(self, out, a, b, op):
        self.pre()
        return self.do(self.eng.tensor_tensor(out=out, in0=a, in1=b, op=op))

    def ts(self, out, a, s1, op0, s2=0.0, op1=None, **kw):
        self.pre()
        if op1 is None:
            return self.do(self.eng.tensor_scalar(out=out, in0=a, scalar1=s1,
                                                  scalar2=s2, op0=op0, **kw))
        return self.do(self.eng.tensor_scalar(out=out, in0=a, scalar1=s1,
                                              scalar2=s2, op0=op0, op1=op1, **kw))

    def act(self, out, in_, func, **kw):
        self.pre()
        return self.do(self.eng.activation(out=out, in_=in_, func=func, **kw))

    def copy(self, out, in_):
        self.pre()
        return self.do(self.eng.tensor_copy(out=out, in_=in_))

    def red(self, out, in_, op=ALU.add, **kw):
        self.pre()
        return self.do(self.eng.tensor_reduce(out=out, in_=in_, axis=AX.X, op=op, **kw))

    def recip(self, out, in_):
        self.pre()
        return self.do(self.eng.reciprocal(out=out, in_=in_))

    def memset(self, ap, v):
        self.pre()
        return self.do(self.eng.memset(ap, v))

    def scan_op(self, out, d0, d1, initial, op0, op1):
        self.pre()
        return self.do(self.eng.tensor_tensor_scan(out=out, data0=d0, data1=d1,
                                                   initial=initial, op0=op0, op1=op1))

    def cc(self, ccsem, kind, op, ins_, outs_, groups=None):
        if groups is None:
            groups = [list(range(NCORE))]
        self.eng.wait_ge(self.sem.h, self.sem.v)
        ins = self.eng.collective_compute(kind, op, replica_groups=groups,
                                          ins=ins_, outs=outs_)
        ins.then_inc(ccsem.h, 16)
        ccsem.v += 16
        return ins


class Prog:
    def __init__(self, name):
        self.nc = bass.Bass("TRN2", target_bir_lowering=False)
        self.ctx = contextlib.ExitStack()
        self.name = name
        self._n = 0

    def uid(self):
        self._n += 1
        return self._n

    def sb(self, name, shape, dtype):
        return self.ctx.enter_context(self.nc.sbuf_tensor(name, shape, dtype))

    def ps(self, name, shape):
        return self.ctx.enter_context(self.nc.psum_tensor(name, shape, F32))

    def dram(self, name, shape, dtype=F32, kind=None):
        if kind is None:
            t = self.nc.dram_tensor(name, shape, dtype)
        else:
            t = self.nc.dram_tensor(name, shape, dtype, kind=kind)
        return t

    def build(self, fn):
        with self.ctx:
            blk = self.ctx.enter_context(self.nc.Block())
            self.pe = Q(self.ctx, self.nc, self.nc.tensor, "pe")
            self.dve = Q(self.ctx, self.nc, self.nc.vector, "dve", selfsync=True)
            self.act = Q(self.ctx, self.nc, self.nc.scalar, "act", selfsync=True)
            self.pool = Q(self.ctx, self.nc, self.nc.gpsimd, "pool", dma_step=16)
            self.sp = Q(self.ctx, self.nc, self.nc.sync, "sp", dma_step=16)
            self.ccsem = Sem(self.ctx, self.nc, "ccsem")

            @blk.tensor
            def _(eng):
                fn(self)
        return self



def load_consts(p, ident=False, ident_bf=False):
    if ident or ident_bf:
        p.eye_in = p.dram("eye_in", [128, 128], F32, kind="ExternalInput").ap()
    if ident:
        p.ident = p.sb("ident", [128, 128], F32)
        p.sp.dma(p.ident[:, :], p.eye_in)
    if ident_bf:
        p.ident_bf = p.sb("ident_bf", [128, 128], BF16)
        p.pool.dma(p.ident_bf[:, :], p.eye_in)
    p.ones_bf = p.sb("ones_col_bf", [128, 1], BF16)
    p.onesf1 = p.sb("ones_row_f", [1, 128], F32)
    ones_in = p.dram("ones_in", [128, 128], F32, kind="ExternalInput").ap()
    p.sp.dma(p.onesf1[0:1, :], ones_in[0:1, :])
    p.pool.dma(p.ones_bf[:, 0:1], ones_in[:, 0:1], allow_slow_non_contiguous=True)


def load_x(p, name="x_fm"):
    x_in = p.dram(name, [D, TOK], F32, kind="ExternalInput").ap()
    x_fm = p.sb(name + "_sb", [128, KC, TOK], F32)
    p.sp.dma(x_fm[:, :, :], x_in.rearrange("(c p) t -> p c t", p=128))
    return x_fm


def store_x(p, x_fm, name="x_out"):
    x_out = p.dram(name, [D, TOK], F32, kind="ExternalOutput").ap()
    p.sp.wait(p.dve)
    p.sp.wait(p.act)
    p.sp.dma(x_out.rearrange("(c p) t -> p c t", p=128), x_fm[:, :, :])
    return x_out


def finish(p):
    p.sp.wait(p.sp)
    p.sp.eng.wait_ge(p.sp.sem.h, p.sp.sem.v)
    p.pool.eng.wait_ge(p.pool.sem.h, p.pool.sem.v)


def load_w(p, host_name, shape_in, sb_shape, rearr=None, dtype=BF16):
    t = p.dram(host_name, list(shape_in), F32, kind="ExternalInput").ap()
    sb_t = p.sb(host_name + "_sb", list(sb_shape), dtype)
    src = t if rearr is None else t.rearrange(*rearr[0], **rearr[1])
    idx = tuple([slice(None)] * len(sb_shape))
    slow = int(np.prod(shape_in)) <= 8192
    p.pool.dma(sb_t[idx], src, allow_slow_non_contiguous=slow)
    return sb_t


def emit_ln(p, x_in, x_ln, scr, psA, psB):
    pe, dve, act, sp = p.pe, p.dve, p.act, p.sp
    xr = x_in.rearrange("(c p) t -> c p t", p=128)
    xc, sqc, stat = scr["xc"], scr["sqc"], scr["stat"]
    xb = scr["xbc"]
    marks = [None, None]
    for c in range(KC):
        b = c % 2
        if marks[b] is not None:
            sp.eng.wait_ge(p.pe.sem.h, marks[b])
        sp.dma(xc[:, b, :], xr[c, :, :])
        act.wait(sp)
        act.wait(pe)
        act.act(sqc[:, :], xc[:, b, :], AF.Square)
        dve.wait(sp)
        dve.wait(pe)
        dve.copy(xb[:, :], xc[:, b, :])
        pe.wait(act)
        pe.wait(dve)
        for n in range(NT):
            pe.mm(psA[0:1, n * 512:(n + 1) * 512], p.ones_bf[:, 0:1],
                  xb[:, n * 512:(n + 1) * 512], start=(c == 0), stop=(c == KC - 1))
        for n in range(NT):
            pe.mm(psA[0:1, TOK + n * 512:TOK + (n + 1) * 512], p.ones_bf[:, 0:1],
                  sqc[:, n * 512:(n + 1) * 512], start=(c == 0), stop=(c == KC - 1))
        marks[b] = p.pe.sem.v
    mean = stat[0:1, 0:TOK]
    rstd = stat[0:1, TOK:2 * TOK]
    mhat = mean
    dve.wait(pe)
    dve.ts(mean, psA[0:1, 0:TOK], 1.0 / D, ALU.mult)
    dve.ts(rstd, psA[0:1, TOK:2 * TOK], 1.0 / D, ALU.mult)
    dve.tt(psA[0:1, 0:TOK], mean, mean, ALU.mult)
    dve.tt(rstd, rstd, psA[0:1, 0:TOK], ALU.subtract)
    act.wait(dve)
    act.act(rstd, rstd, AF.Sqrt, bias=scr["eps1"][0:1, 0:1])
    dve.wait(act)
    dve.recip(rstd, rstd)
    dve.tt(mhat, mean, rstd, ALU.mult)
    for n in range(NT):
        pe.wait(dve)
        pe.mm(psB[:, 0:512], p.onesf1[0:1, :], rstd[0:1, n * 512:(n + 1) * 512])
        pe.mm(psB[:, 512:1024], p.onesf1[0:1, :], mhat[0:1, n * 512:(n + 1) * 512])
        sl = slice(n * 512, (n + 1) * 512)
        for c in range(KC):
            b = c % 2
            sp.wait(p.dve)
            sp.dma(xc[:, b, 0:512], xr[c, :, sl])
            dve.wait(sp)
            dve.wait(pe)
            dve.tt(xc[:, b, 512:1024], xc[:, b, 0:512], psB[:, 0:512], ALU.mult)
            dve.tt(x_ln[:, c, sl], xc[:, b, 512:1024], psB[:, 512:1024], ALU.subtract)


def ln_scratch(p, tag=""):
    return {
        "xc": p.sb("lnxc" + tag, [128, 2, TOK], F32),
        "xbc": p.sb("lnxb" + tag, [128, TOK], BF16),
        "sqc": p.sb("lnsq" + tag, [128, TOK], BF16),
        "stat": p.sb("lnst" + tag, [1, 2 * TOK], F32),
        "eps1": p.sb("lneps" + tag, [1, 1], F32),
    }


def emit_proj(p, psum, w_sb, x_ap, dout, evac, nk=KC, tokn=TOK, nsl=512, banks=None):
    pe = p.pe
    nj = dout // 128
    nn = tokn // nsl
    if banks is None:
        banks = psum.shape[1] // nsl
    marks = {}
    g = 0
    for j in range(nj):
        for n in range(nn):
            bank = g % banks
            if bank in marks:
                q, v = marks[bank]
                pe.wait(q, v)
            pap = psum[:, bank * nsl:(bank + 1) * nsl]
            for k in range(nk):
                pe.mm(pap, w_sb[:, k, j * 128:(j + 1) * 128], x_ap(k, n),
                      start=(k == 0), stop=(k == nk - 1))
            q = evac(j, n, pap)
            marks[bank] = (q, q.sem.v)
            g += 1


def bcast_row(p, psum_ap, row_ap, nsl=512):
    n = row_ap.shape[-1]
    for i in range(0, n, nsl):
        w = min(nsl, n - i)
        p.pe.mm(psum_ap[:, i:i + w], p.onesf1[0:1, :], row_ap[0:1, i:i + w])



def build_attn_a0_pre():
    p = Prog("attn_a0_pre")

    def emit(p):
        pe, dve, act, pool, sp = p.pe, p.dve, p.act, p.pool, p.sp
        load_consts(p)
        x_in = p.dram("x_fm", [D, TOK], F32, kind="ExternalInput").ap()
        wq = load_w(p, "wq16", (D, 16), (128, KC, 16), (("(c p) h -> p c h",), dict(p=128)))
        wk = load_w(p, "wk16", (D, 16), (128, KC, 16), (("(c p) h -> p c h",), dict(p=128)))
        wbuf = p.sb("wbuf", [128, KC, D], BF16)
        wv_in = p.dram("wv", [D, D], F32, kind="ExternalInput").ap()
        pool.dma(wbuf[:, :, :], wv_in.rearrange("(c p) h -> p c h", p=128))
        rep_in = p.dram("rep", [KC, 16, 128], F32, kind="ExternalInput").ap()
        repb = p.sb("repb_sb", [16, KC, 128], BF16)
        pool.dma(repb[:, :, :], rep_in.rearrange("c h p -> h c p"))

        x_ln = p.sb("x_ln", [128, KC, TOK], BF16)
        scr = ln_scratch(p)
        PS = p.ps("PS", [128, 4096])
        psA = PS[0:1, 0:2 * TOK]
        psB = PS[:, 0:1024]
        psQ = PS[0:16, 0:2 * TOK]
        psP = PS[:, 0:TOK]
        psP2 = PS[:, TOK:2 * TOK]

        dve.wait(sp); dve.wait(pool)
        dve.memset(scr["eps1"][0:1, 0:1], 1e-5)
        pe.wait(sp); pe.wait(pool); pe.wait(dve)
        emit_ln(p, x_in, x_ln, scr, psA, psB)

        pe.wait(dve)
        for n in range(NT):
            for c in range(KC):
                pe.mm(psQ[:, n * 512:(n + 1) * 512], wq[:, c, :],
                      x_ln[:, c, n * 512:(n + 1) * 512], start=(c == 0), stop=(c == KC - 1))
        for n in range(NT):
            for c in range(KC):
                pe.mm(psQ[:, TOK + n * 512:TOK + (n + 1) * 512], wk[:, c, :],
                      x_ln[:, c, n * 512:(n + 1) * 512], start=(c == 0), stop=(c == KC - 1))
        tn = p.sb("tn16", [16, 2 * TOK], F32)
        cq = p.sb("cq", [16, TOK], BF16)
        sq_ = p.sb("sq_", [16, TOK], BF16)
        ckr = p.sb("ckr", [16, TOK], BF16)
        skr = p.sb("skr", [16, TOK], BF16)
        ck = p.sb("ckc", [16, TOK], F32)
        sk = p.sb("skc", [16, TOK], F32)
        denb = p.sb("denb", [16, TOK], BF16)
        hpi = p.sb("halfpi", [16, 1], F32)
        dve.memset(hpi[:, 0:1], PI / 2.0)
        act.wait(pe); act.wait(dve)
        act.act(tn[:, 0:2 * TOK], psQ, AF.Tanh)
        act.act(cq[0:16, :], tn[:, 0:TOK], AF.Sin, scale=PI / 4.0, bias=hpi[:, 0:1])
        act.act(sq_[0:16, :], tn[:, 0:TOK], AF.Sin, scale=PI / 4.0)
        act.act(ckr[0:16, :], tn[:, TOK:2 * TOK], AF.Sin, scale=PI / 4.0, bias=hpi[:, 0:1])
        act.act(skr[0:16, :], tn[:, TOK:2 * TOK], AF.Sin, scale=PI / 4.0)

        v_bf = p.sb("v_bf", [128, KC, TOK], BF16)
        pe.wait(act)

        def evac_v(j, n, pap):
            dve.wait(pe)
            dve.copy(v_bf[:, j, n * 512:(n + 1) * 512], pap)
            return dve
        emit_proj(p, PS[:, 0:2 * TOK], wbuf,
                  lambda k, n: x_ln[:, k, n * 512:(n + 1) * 512], D, evac_v)

        ckv = p.sb("ckv_c", [128, TOK], F32)
        skv = p.sb("skv_c", [128, TOK], F32)
        tot = p.sb("tot", [128, 20], F32)
        onesw = p.sb("onesw", [128, TOK], BF16)
        dve.memset(onesw[:, :], 1.0)
        dve.memset(tot[:, 16:20], 0.0)
        for c in range(KC):
            pe.wait(dve)
            for n in range(NT):
                pe.mm(psP[:, n * 512:(n + 1) * 512], repb[:, c, :],
                      ckr[0:16, n * 512:(n + 1) * 512])
            for n in range(NT):
                pe.mm(psP2[:, n * 512:(n + 1) * 512], repb[:, c, :],
                      skr[0:16, n * 512:(n + 1) * 512])
            dve.wait(pe)
            dve.tt(ckv[:, :], v_bf[:, c, :], psP, ALU.mult)
            dve.tt(skv[:, :], v_bf[:, c, :], psP2, ALU.mult)
            dve.red(tot[:, c:c + 1], ckv[:, :])
            dve.red(tot[:, KC + c:KC + c + 1], skv[:, :])
        dve.red(tot[0:16, 16:17], ckr[0:16, :])
        dve.red(tot[0:16, 17:18], skr[0:16, :])

        trig_out = p.dram("trig_out", [4, 16, TOK], BF16, kind="ExternalOutput").ap()
        v_out = p.dram("v_out", [128, KC * TOK], BF16, kind="ExternalOutput").ap()
        tot_out = p.dram("tot_out", [128, 20], F32, kind="ExternalOutput").ap()
        sp.wait(dve)
        sp.wait(act)
        sp.dma(trig_out[0], cq[0:16, :])
        sp.dma(trig_out[1], sq_[0:16, :])
        sp.dma(trig_out[2], ckr[0:16, :])
        sp.dma(trig_out[3], skr[0:16, :])
        sp.dma(v_out.rearrange("p (c t) -> p c t", c=KC), v_bf[:, :, :])
        sp.dma(tot_out, tot[:, :])
        finish(p)

    return p.build(emit)


def build_attn_a0_main():
    p = Prog("attn_a0_main")

    def emit(p):
        pe, dve, act, pool, sp = p.pe, p.dve, p.act, p.pool, p.sp
        load_consts(p)
        x_in = p.dram("x_fm", [D, TOK], F32, kind="ExternalInput").ap()
        xr = x_in.rearrange("(c p) t -> c p t", p=128)
        trig_in = p.dram("trig", [4, 16, TOK], BF16, kind="ExternalInput").ap()
        v_in = p.dram("v_in", [128, KC * TOK], BF16, kind="ExternalInput").ap()
        offs_in = p.dram("offs_in", [128, 20], F32, kind="ExternalInput").ap()
        wo_in = p.dram("wo2", [D, D], F32, kind="ExternalInput").ap()
        wbuf = p.sb("wbuf", [128, KC, D], BF16)
        pool.dma(wbuf[:, :, :], wo_in.rearrange("(c p) h -> p c h", p=128))
        rep_in = p.dram("rep", [KC, 16, 128], F32, kind="ExternalInput").ap()
        repb = p.sb("repb_sb", [16, KC, 128], BF16)
        pool.dma(repb[:, :, :], rep_in.rearrange("c h p -> h c p"))

        cq = p.sb("cq", [16, TOK], BF16)
        sq_ = p.sb("sq_", [16, TOK], BF16)
        ckr = p.sb("ckr", [16, TOK], BF16)
        skr = p.sb("skr", [16, TOK], BF16)
        sp.dma(cq[0:16, :], trig_in[0])
        sp.dma(sq_[0:16, :], trig_in[1])
        sp.dma(ckr[0:16, :], trig_in[2])
        sp.dma(skr[0:16, :], trig_in[3])
        v_bf = p.sb("v_bf", [128, KC, TOK], BF16)
        sp.dma(v_bf[:, :, :], v_in.rearrange("p (c t) -> p c t", c=KC))
        offs = p.sb("offs", [128, 20], F32)
        sp.dma(offs[:, :], offs_in)

        PS = p.ps("PS", [128, 4096])
        psP = PS[:, 0:TOK]
        psP2 = PS[:, TOK:2 * TOK]
        ck = p.sb("ckc", [16, TOK], F32)
        sk = p.sb("skc", [16, TOK], F32)
        denb = p.sb("denb", [16, TOK], BF16)
        tn = p.sb("tn16", [16, 2 * TOK], F32)
        ckv = p.sb("ckv_c", [128, TOK], F32)
        skv = p.sb("skv_c", [128, TOK], F32)
        onesw = p.sb("onesw", [128, TOK], BF16)
        dve.wait(sp)
        dve.wait(pool)
        dve.memset(onesw[:, :], 1.0)
        dve.scan_op(ck[0:16, :], onesw[0:16, :], ckr[0:16, :], offs[0:16, 16:17],
                    ALU.mult, ALU.add)
        dve.scan_op(sk[0:16, :], onesw[0:16, :], skr[0:16, :], offs[0:16, 17:18],
                    ALU.mult, ALU.add)
        den = tn[:, 0:TOK]; t2 = tn[:, TOK:2 * TOK]
        dve.tt(den, cq[0:16, :], ck[0:16, :], ALU.mult)
        dve.tt(t2, sq_[0:16, :], sk[0:16, :], ALU.mult)
        dve.tt(den, den, t2, ALU.add)
        dve.ts(den, den, 1e-6, ALU.add)
        dve.recip(den, den)
        dve.copy(denb[0:16, :], den)

        hd = p.sb("hd", [128, KC, TOK], BF16)
        htmp = p.sb("htmp", [128, TOK], F32)
        for c in range(KC):
            pe.wait(dve)
            for n in range(NT):
                pe.mm(psP[:, n * 512:(n + 1) * 512], repb[:, c, :],
                      ckr[0:16, n * 512:(n + 1) * 512])
            for n in range(NT):
                pe.mm(psP2[:, n * 512:(n + 1) * 512], repb[:, c, :],
                      skr[0:16, n * 512:(n + 1) * 512])
            dve.wait(pe)
            dve.tt(ckv[:, :], v_bf[:, c, :], psP, ALU.mult)
            dve.tt(skv[:, :], v_bf[:, c, :], psP2, ALU.mult)
            dve.scan_op(ckv[:, :], onesw[:, :], ckv[:, :], offs[:, c:c + 1],
                        ALU.mult, ALU.add)
            dve.scan_op(skv[:, :], onesw[:, :], skv[:, :], offs[:, KC + c:KC + c + 1],
                        ALU.mult, ALU.add)
            pe.wait(dve)
            for n in range(NT):
                pe.mm(psP[:, n * 512:(n + 1) * 512], repb[:, c, :],
                      cq[0:16, n * 512:(n + 1) * 512])
            for n in range(NT):
                pe.mm(psP2[:, n * 512:(n + 1) * 512], repb[:, c, :],
                      sq_[0:16, n * 512:(n + 1) * 512])
            dve.wait(pe)
            dve.tt(htmp[:, :], ckv[:, :], psP, ALU.mult)
            dve.tt(ckv[:, :], skv[:, :], psP2, ALU.mult)
            dve.tt(htmp[:, :], htmp[:, :], ckv[:, :], ALU.add)
            pe.wait(dve)
            for n in range(NT):
                pe.mm(psP[:, n * 512:(n + 1) * 512], repb[:, c, :],
                      denb[0:16, n * 512:(n + 1) * 512])
            dve.wait(pe)
            dve.tt(hd[:, c, :], htmp[:, :], psP, ALU.mult)

        x_out = p.dram("x_out", [D, TOK], F32, kind="ExternalOutput").ap()
        xo = x_out.rearrange("(c p) t -> c p t", p=128)
        xc = p.sb("xcm", [128, 2, TOK], F32)
        res = htmp
        pe.wait(pool)
        pe.wait(dve)

        def evac_o(j, n, pap):
            b = n % 2
            sl = slice(n * 512, (n + 1) * 512)
            sp.wait(dve)
            sp.dma(xc[:, b, n * 512:(n + 1) * 512], xr[j, :, sl])
            dve.wait(sp)
            dve.wait(pe)
            dve.tt(res[:, sl], xc[:, b, sl], pap, ALU.add)
            if n == NT - 1:
                sp.wait(dve)
                sp.dma(xo[j, :, :], res[:, :])
            return dve
        emit_proj(p, PS[:, 0:2 * TOK], wbuf,
                  lambda k, n: hd[:, k, n * 512:(n + 1) * 512], D, evac_o)
        finish(p)

    return p.build(emit)



def build_attn_b0():
    p = Prog("attn_b0")
    S = 4096

    def emit(p):
        pe, dve, act, pool, sp = p.pe, p.dve, p.act, p.pool, p.sp
        load_consts(p)
        x_in = p.dram("x_fm", [D, TOK], F32, kind="ExternalInput").ap()
        xr = x_in.rearrange("(c p) t -> c p t", p=128)
        kT_in = p.dram("kT", [D, S], F32, kind="ExternalInput").ap()
        kTr = kT_in.rearrange("(c p) s -> p c s", p=128)
        vW_in = p.dram("vW", [S, D], F32, kind="ExternalInput").ap()

        x_ln = p.sb("x_ln", [128, KC, TOK], BF16)
        scr = ln_scratch(p)
        PS = p.ps("PS", [128, 4096])
        psA = PS[0:1, 0:2 * TOK]
        psB = PS[:, 0:1024]
        psl1 = PS[0:1, 3072:3584]
        l2row = p.sb("l2row", [1, TOK], F32)

        dve.wait(sp); dve.wait(pool)
        dve.memset(scr["eps1"][0:1, 0:1], 1e-5)
        pe.wait(sp); pe.wait(pool); pe.wait(dve)
        emit_ln(p, x_in, x_ln, scr, psA, psB)

        kt_sb = p.sb("kt_sb", [128, KC, 512], BF16)
        v_sb = p.sb("v_sb", [128, 4, D], BF16)
        g_sb = p.sb("g_sb", [128, 4, TOK], BF16)
        sq_sb = p.sb("sq_sb", [128, TOK], BF16)
        acc = p.sb("acc", [128, KC, TOK], F32)
        pe.wait(dve)
        for sb_i in range(8):
            pool.wait(pe)
            pool.dma(kt_sb[:, :, :], kTr[:, :, sb_i * 512:(sb_i + 1) * 512])
            pool.dma(v_sb[:, :, :],
                     vW_in[sb_i * 512:(sb_i + 1) * 512, :].rearrange("(a p) d -> p a d", p=128))
            pe.wait(pool)
            for st in range(4):
                pscore = PS[:, 0:TOK]
                if st > 0 or sb_i > 0:
                    pe.wait(act)
                for n in range(NT):
                    for k in range(KC):
                        pe.mm(pscore[:, n * 512:(n + 1) * 512],
                              kt_sb[:, k, st * 128:(st + 1) * 128],
                              x_ln[:, k, n * 512:(n + 1) * 512],
                              start=(k == 0), stop=(k == KC - 1))
                act.wait(pe)
                act.act(g_sb[:, st, :], pscore, AF.Gelu)
                act.act(sq_sb[:, :], g_sb[:, st, :], AF.Square)
                for n in range(NT):
                    pe.wait(act)
                    pe.wait(dve)
                    pe.mm(psl1, p.ones_bf[:, 0:1], sq_sb[:, n * 512:(n + 1) * 512])
                    dve.wait(pe)
                    sl = slice(n * 512, (n + 1) * 512)
                    if sb_i == 0 and st == 0:
                        dve.copy(l2row[0:1, sl], psl1)
                    else:
                        dve.tt(l2row[0:1, sl], l2row[0:1, sl], psl1, ALU.add)
            for j in range(KC):
                for n in range(NT):
                    bank = (j * NT + n) % 2
                    pav = PS[:, TOK + bank * 512:TOK + (bank + 1) * 512]
                    if j * NT + n >= 2:
                        pe.wait(dve)
                    for st in range(4):
                        pe.mm(pav, v_sb[:, st, j * 128:(j + 1) * 128],
                              g_sb[:, st, n * 512:(n + 1) * 512],
                              start=(st == 0), stop=(st == 3))
                    dve.wait(pe)
                    sl = slice(n * 512, (n + 1) * 512)
                    if sb_i == 0:
                        dve.copy(acc[:, j, sl], pav)
                    else:
                        dve.tt(acc[:, j, sl], acc[:, j, sl], pav, ALU.add)
        stat = scr["stat"]
        sc_row = stat[0:1, 0:TOK]
        dve.wait(pe)
        act.wait(pe)
        act.act(sc_row, l2row[0:1, :], AF.Sqrt)
        dve.wait(act)
        dve.recip(sc_row, sc_row)
        dve.ts(sc_row, sc_row, 64.0, ALU.mult)
        x_out = p.dram("x_out", [D, TOK], F32, kind="ExternalOutput").ap()
        xo = x_out.rearrange("(c p) t -> c p t", p=128)
        xc = scr["xc"]
        res = p.sb("res_c", [128, TOK], F32)
        for n in range(NT):
            sl = slice(n * 512, (n + 1) * 512)
            pe.wait(dve)
            pe.mm(psB[:, 0:512], p.onesf1[0:1, :], sc_row[0:1, sl])
            for c in range(KC):
                b = c % 2
                sp.wait(dve)
                sp.dma(xc[:, b, 0:512], xr[c, :, sl])
                dve.wait(sp)
                dve.wait(pe)
                dve.tt(res[:, sl], acc[:, c, sl], psB[:, 0:512], ALU.mult)
                dve.tt(res[:, sl], res[:, sl], xc[:, b, 0:512], ALU.add)
                sp.wait(dve)
                sp.dma(xo[c, :, sl], res[:, sl])
        finish(p)

    return p.build(emit)


def prep_attn_b0(inp, x_fms):
    kp = np.asarray(inp["L0_attn_b_key_param_tokens"], dtype=np.float32)
    vp = np.asarray(inp["L0_attn_b_value_param_tokens"], dtype=np.float32)
    kT = np.ascontiguousarray(kp.T)
    ones = np.ones((128, 128), dtype=np.float32)
    return [dict(x_fm=x_fms[c], kT=kT, vW=vp, ones_in=ones) for c in range(NCORE)]



def build_ffn0_pre():
    p = Prog("ffn0_pre")

    def emit(p):
        pe, dve, act, pool, sp = p.pe, p.dve, p.act, p.pool, p.sp
        load_consts(p)
        x_in = p.dram("x_fm", [D, TOK], F32, kind="ExternalInput").ap()
        rt = load_w(p, "rt", (D, 1), (128, KC, 1), (("(c p) o -> p c o",), dict(p=128)))

        x_ln = p.sb("x_ln", [128, KC, TOK], BF16)
        scr = ln_scratch(p)
        PS = p.ps("PS", [128, 4096])
        psA = PS[0:1, 0:2 * TOK]
        psB = PS[:, 0:1024]

        dve.wait(sp); dve.wait(pool)
        dve.memset(scr["eps1"][0:1, 0:1], 1e-5)
        pe.wait(sp); pe.wait(pool); pe.wait(dve)
        emit_ln(p, x_in, x_ln, scr, psA, psB)

        pe.wait(dve)
        for n in range(NT):
            for k in range(KC):
                pe.mm(psA[0:1, n * 512:(n + 1) * 512], rt[:, k, 0:1],
                      x_ln[:, k, n * 512:(n + 1) * 512], start=(k == 0), stop=(k == KC - 1))
        s_sb = p.sb("s_sb", [1, TOK], F32)
        dve.wait(pe)
        dve.copy(s_sb[0:1, :], psA[0:1, 0:TOK])

        xln_out = p.dram("xln_out", [128, KC * TOK], BF16, kind="ExternalOutput").ap()
        s_out = p.dram("s_out", [1, TOK], F32, kind="ExternalOutput").ap()
        sp.wait(dve)
        sp.dma(xln_out.rearrange("p (c t) -> p c t", c=KC), x_ln[:, :, :])
        sp.dma(s_out, s_sb[0:1, :])
        finish(p)

    return p.build(emit)


def build_ffn0_main():
    p = Prog("ffn0_main")
    LOGK = float(np.log(1152.0))
    A1 = float(np.log(1152.0) - np.log(4096.0))

    def emit(p):
        pe, dve, act, pool, sp = p.pe, p.dve, p.act, p.pool, p.sp
        load_consts(p)
        x_in = p.dram("x_fm", [D, TOK], F32, kind="ExternalInput").ap()
        xr = x_in.rearrange("(c p) t -> c p t", p=128)
        xln_in = p.dram("xln_in", [128, KC * TOK], BF16, kind="ExternalInput").ap()
        s128_in = p.dram("s128", [128, 32], F32, kind="ExternalInput").ap()
        s2_in = p.dram("s2", [2, TOK], F32, kind="ExternalInput").ap()
        hsel_in = p.dram("hsel", [2, 1], F32, kind="ExternalInput").ap()
        lw1 = load_w(p, "lw1", (D, 512), (128, KC, 512), (("(c p) o -> p c o",), dict(p=128)))
        lw2 = load_w(p, "lw2", (512, D), (128, 4, D), (("(c p) o -> p c o",), dict(p=128)))
        gam = load_w(p, "gams", (D, 4), (128, KC, 4), (("(c p) o -> p c o",), dict(p=128)), dtype=F32)
        lb1 = load_w(p, "lb1", (512, 1), (128, 4, 1), (("(c p) o -> p c o",), dict(p=128)), dtype=F32)
        hb1 = load_w(p, "hb1", (4096, 1), (128, 32, 1), (("(c p) o -> p c o",), dict(p=128)), dtype=F32)
        hw1_in = p.dram("hw1", [D, 4096], F32, kind="ExternalInput").ap()
        hw2_in = p.dram("hw2", [4096, D], F32, kind="ExternalInput").ap()

        scr = ln_scratch(p)
        x_ln = p.sb("x_ln", [128, KC, TOK], BF16)
        sp.dma(x_ln[:, :, :], xln_in.rearrange("p (c t) -> p c t", c=KC))
        s128 = p.sb("s128_sb", [128, 32], F32)
        sp.dma(s128[:, :], s128_in)
        s2 = scr["xc"][0:2, 0, 0:TOK]
        sp.dma(s2, s2_in)
        hsel = p.sb("hsel_sb", [2, 1], F32)
        sp.dma(hsel[:, :], hsel_in)
        PS = p.ps("PS", [128, 4096])
        psA = PS[0:1, 0:2 * TOK]
        psB = PS[:, 0:1024]

        ones128 = p.sb("ones128", [128, 1], F32)
        asb = p.sb("asb", [1, 2], F32)
        a128 = p.sb("a128", [128, 1], F32)
        u_cd = p.sb("u_cd", [128, 32], F32)
        red1 = p.sb("red1", [128, 1], F32)
        psb1 = PS[:, 2048:2049]
        psT = PS[0:1, 3072:3073]
        dve.wait(sp); dve.wait(pool)
        dve.memset(ones128[:, :], 1.0)
        dve.memset(asb[0:1, 0:1], A1)
        for it in range(49):
            pe.wait(dve)
            pe.mm(psb1, p.onesf1[0:1, :], asb[0:1, 0:1])
            dve.wait(pe)
            dve.copy(a128[:, 0:1], psb1)
            dve.ts(u_cd[:, :], s128[:, :], a128[:, 0:1], ALU.add, 0.0, ALU.min)
            act.wait(dve)
            act.act(u_cd[:, :], u_cd[:, :], AF.Exp)
            dve.wait(act)
            dve.red(red1[:, 0:1], u_cd[:, :])
            pe.wait(dve)
            pe.mm(psT, ones128[:, 0:1], red1[:, 0:1])
            act.wait(pe)
            act.act(asb[0:1, 1:2], psT, AF.Ln)
            dve.wait(act)
            dve.tt(asb[0:1, 0:1], asb[0:1, 0:1], asb[0:1, 1:2], ALU.subtract)
            dve.ts(asb[0:1, 0:1], asb[0:1, 0:1], LOGK, ALU.add)

        pos2 = scr["xc"][0:2, 1, 0:TOK]
        cum2 = s2
        a2v = p.sb("a2v", [2, 1], F32)
        tot2 = p.sb("tot2", [2, 1], F32)
        init2 = p.sb("init2", [2, 1], F32)
        shifL = p.sb("shifL", [2, 2], F32)
        mrow = scr["stat"][0:1, TOK:2 * TOK]
        maskb = p.sb("maskb", [128, TOK], BF16)
        dve.memset(shifL[:, :], 0.0)
        dve.memset(shifL[0:1, 1:2], 1.0)
        pe.wait(dve)
        pe.mm(psb1, p.onesf1[0:1, :], asb[0:1, 0:1])
        dve.wait(pe)
        dve.copy(a2v[0:2, 0:1], psb1[0:2, 0:1])
        dve.ts(pos2, s2, a2v[0:2, 0:1], ALU.add, 0.0, ALU.is_gt)
        dve.red(tot2[:, 0:1], pos2)
        pe.wait(dve)
        pe.mm(PS[0:2, 3074:3075], shifL[0:2, 0:2], tot2[0:2, 0:1])
        dve.wait(pe)
        dve.copy(init2[:, 0:1], PS[0:2, 3074:3075])
        dve.scan_op(cum2, pos2, pos2, init2[:, 0:1], ALU.add, ALU.max)
        dve.ts(cum2, cum2, 1024.5, ALU.is_le)
        dve.tt(pos2, pos2, cum2, ALU.mult)
        pe.wait(dve)
        for n in range(NT):
            pe.mm(psA[0:1, n * 512:(n + 1) * 512], hsel[0:2, 0:1],
                  pos2[:, n * 512:(n + 1) * 512])
        dve.wait(pe)
        dve.copy(mrow, psA[0:1, 0:TOK])
        pe.wait(dve)
        for n in range(NT):
            pe.mm(psB[:, 0:512], p.onesf1[0:1, :], mrow[0:1, n * 512:(n + 1) * 512])
            dve.wait(pe)
            dve.copy(maskb[:, n * 512:(n + 1) * 512], psB[:, 0:512])
            if n < NT - 1:
                pe.wait(dve)

        y = p.sb("ybuf", [128, KC, TOK], BF16)
        h_n = p.sb("h_n", [128, 16384], BF16)
        h32 = h_n.ap().rearrange("p (a b) -> p a b", a=32)
        h4 = h_n.ap().rearrange("p (a b) -> p a b", a=8)[:, 0:4, :]
        wst = p.sb("wstream", [128, 8192], BF16)
        w8 = wst.ap().rearrange("p (c o) -> p c o", c=8)
        tmp5 = scr["xc"]
        hacc = p.dram("hacc_f0", [KC, 128, TOK], F32)

        def rms_apply(gam_col):
            sqx = scr["sqc"]
            for c in range(KC):
                act.wait(dve); act.wait(pe)
                act.act(sqx[:, :], x_ln[:, c, :], AF.Square)
                pe.wait(act)
                for n in range(NT):
                    pe.mm(psA[0:1, n * 512:(n + 1) * 512], p.ones_bf[:, 0:1],
                          sqx[:, n * 512:(n + 1) * 512], start=(c == 0), stop=(c == KC - 1))
            row = scr["stat"][0:1, 0:TOK]
            act.wait(pe)
            act.act(row, psA[0:1, 0:TOK], AF.Sqrt)
            dve.wait(act)
            dve.ts(row, row, 1e-12, ALU.max)
            dve.recip(row, row)
            dve.ts(row, row, 32.0, ALU.mult)
            for n in range(NT):
                sl = slice(n * 512, (n + 1) * 512)
                pe.wait(dve)
                pe.mm(psB[:, 0:512], p.onesf1[0:1, :], row[0:1, sl])
                dve.wait(pe)
                for c in range(KC):
                    dve.tt(tmp5[:, 0, 0:512], x_ln[:, c, sl], psB[:, 0:512], ALU.mult)
                    dve.ts(y[:, c, sl], tmp5[:, 0, 0:512], gam[:, c, gam_col:gam_col + 1],
                           ALU.mult)

        hw1r = hw1_in.rearrange("(c p) o -> p c o", p=128)
        hw2r = hw2_in.rearrange("(c p) o -> p c o", p=128)

        rms_apply(1)
        for n in range(NT):
            nsl = slice(n * 512, (n + 1) * 512)
            for q in range(4):
                pool.wait(pe)
                pool.dma(w8[:, :, :], hw1r[:, :, q * 1024:(q + 1) * 1024])
                pe.wait(pool)
                for j in range(8):
                    jj = q * 8 + j
                    pj = PS[:, TOK + (j % 2) * 512:TOK + (j % 2) * 512 + 512]
                    pe.wait(act)
                    for k in range(KC):
                        pe.mm(pj, w8[:, k, j * 128:(j + 1) * 128], y[:, k, nsl],
                              start=(k == 0), stop=(k == KC - 1))
                    act.wait(pe)
                    act.act(h32[:, jj, :], pj, AF.Gelu, bias=hb1[:, jj, 0:1])
            for q in range(4):
                pool.wait(pe)
                pool.dma(w8[:, :, :], hw2r[:, q * 8:(q + 1) * 8, :])
                pe.wait(pool)
                if q == 0:
                    pe.wait(dve)
                    pe.wait(act)
                for j in range(KC):
                    for k in range(KC):
                        pe.mm(PS[:, j * 512:(j + 1) * 512], w8[:, k, j * 128:(j + 1) * 128],
                              h32[:, q * 8 + k, :],
                              start=(q == 0 and k == 0), stop=(q == 3 and k == KC - 1))
            for j in range(KC):
                act.wait(pe)
                act.wait(dve)
                act.act(tmp5[:, 1, 0:512], PS[:, j * 512:(j + 1) * 512], AF.Identity,
                        bias=gam[:, j, 3:4])
                dve.wait(act)
                dve.wait(sp)
                dve.tt(tmp5[:, 0, 512:1024], tmp5[:, 1, 0:512], maskb[:, nsl], ALU.mult)
                sp.wait(dve)
                sp.dma(hacc.rearrange("c p t -> c p t")[j, :, nsl], tmp5[:, 0, 512:1024])
            pe.wait(dve)

        rms_apply(0)

        def evac_l1(j, n, pap):
            act.wait(pe)
            act.act(h4[:, j, n * 512:(n + 1) * 512], pap, AF.Gelu, bias=lb1[:, j, 0:1])
            return act
        pe.wait(act); pe.wait(dve)
        emit_proj(p, PS[:, 0:2 * TOK], lw1,
                  lambda k, n: y[:, k, n * 512:(n + 1) * 512], 512, evac_l1)

        x_out = p.dram("x_out", [D, TOK], F32, kind="ExternalOutput").ap()
        xo = x_out.rearrange("(c p) t -> c p t", p=128)
        res = h_n.ap().bitcast(F32)[:, 4096:6144]

        def evac_l2(j, n, pap):
            sl = slice(n * 512, (n + 1) * 512)
            act.wait(pe)
            act.wait(dve)
            act.act(tmp5[:, 1, 0:512], pap, AF.Identity, bias=gam[:, j, 2:3])
            b = n % 2
            sp.wait(dve)
            sp.dma(tmp5[:, b, 1024:1536], xr[j, :, sl])
            sp.dma(tmp5[:, b, 1536:2048], hacc[j, :, sl])
            dve.wait(act)
            dve.wait(sp)
            dve.tt(res[:, sl], tmp5[:, 1, 0:512], tmp5[:, b, 1024:1536], ALU.add)
            dve.tt(res[:, sl], res[:, sl], tmp5[:, b, 1536:2048], ALU.add)
            sp.wait(dve)
            sp.dma(xo[j, :, sl], res[:, sl])
            return dve
        pe.wait(act)
        emit_proj(p, PS[:, 0:2 * TOK], lw2,
                  lambda k, n: h4[:, k, n * 512:(n + 1) * 512], D, evac_l2, nk=4)
        finish(p)

    return p.build(emit)


def prep_ffn0_pre(inp, x_fms):
    g = lambda k: np.asarray(inp[k], dtype=np.float32)
    ones = np.ones((128, 128), dtype=np.float32)
    return [dict(x_fm=x_fms[c], rt=g("L0_ffn_routing_token").reshape(D, 1),
                 ones_in=ones) for c in range(NCORE)]


def prep_ffn0_main(inp, x_fms, xls, ss):
    g = lambda k: np.asarray(inp[k], dtype=np.float32)
    gams = np.stack([g("L0_ffn_light_gamma"), g("L0_ffn_heavy_gamma"),
                     g("L0_ffn_light_b2"), g("L0_ffn_heavy_b2")], axis=1)
    ones = np.ones((128, 128), dtype=np.float32)
    maps = []
    for c in range(NCORE):
        b, hf = c // 2, c % 2
        s_full = np.concatenate([np.asarray(ss[2 * b], np.float32).reshape(TOK),
                                 np.asarray(ss[2 * b + 1], np.float32).reshape(TOK)])
        hsel = np.zeros((2, 1), np.float32)
        hsel[hf, 0] = 1.0
        maps.append(dict(x_fm=x_fms[c], xln_in=np.asarray(xls[c]),
                         s128=np.ascontiguousarray(s_full.reshape(128, 32)),
                         s2=np.ascontiguousarray(s_full.reshape(2, TOK)),
                         hsel=hsel,
                         lw1=g("L0_ffn_light_w1"), lw2=g("L0_ffn_light_w2"),
                         gams=gams, lb1=g("L0_ffn_light_b1").reshape(512, 1),
                         hb1=g("L0_ffn_heavy_b1").reshape(4096, 1),
                         hw1=g("L0_ffn_heavy_w1"), hw2=g("L0_ffn_heavy_w2"),
                         ones_in=ones))
    return maps



def build_attn_a1_pre():
    p = Prog("attn_a1_pre")

    def emit(p):
        pe, dve, act, pool, sp = p.pe, p.dve, p.act, p.pool, p.sp
        load_consts(p, ident=True)
        x_in = p.dram("x_fm", [D, TOK], F32, kind="ExternalInput").ap()
        wq = load_w(p, "wq48", (D, 48), (128, KC, 48), (("(c p) h -> p c h",), dict(p=128)))
        wk = load_w(p, "wk48", (D, 48), (128, KC, 48), (("(c p) h -> p c h",), dict(p=128)))
        wv = load_w(p, "wv1", (D, D), (128, KC, D), (("(c p) h -> p c h",), dict(p=128)))
        dirs = load_w(p, "dirs48", (48, 1), (48, 1), None, dtype=F32)
        t3 = load_w(p, "t3mat", (48, 16), (48, 16), None, dtype=BF16)
        selh = load_w(p, "selh", (80, 64), (80, 64), None, dtype=BF16)
        scale_in = p.dram("scale1", [1, 1], F32, kind="ExternalInput").ap()
        sc0 = p.sb("sc0", [1, 1], F32)
        sc1 = p.sb("sc1", [48, 1], F32)
        sp.dma(sc0[:, :], scale_in)
        qh_out = p.dram("qh_out", [4, 16, TOK], BF16, kind="ExternalOutput").ap()
        kh_out = p.dram("kh_out", [4, 16, TOK], BF16, kind="ExternalOutput").ap()
        va_out = p.dram("va_out", [TOK, 1040], BF16, kind="ExternalOutput").ap()

        x_ln = p.sb("x_ln", [128, KC, TOK], BF16)
        scr = ln_scratch(p)
        PS = p.ps("PS", [128, 4096])
        psA = PS[0:1, 0:2 * TOK]
        psB = PS[:, 0:1024]

        dve.wait(sp); dve.wait(pool)
        dve.memset(scr["eps1"][0:1, 0:1], 1e-5)
        pe.wait(sp); pe.wait(pool); pe.wait(dve)
        emit_ln(p, x_in, x_ln, scr, psA, psB)

        pe.wait(dve); pe.wait(sp)
        pe.mm(PS[0:48, 0:1], p.onesf1[0:1, 0:48], sc0[0:1, 0:1])
        dve.wait(pe)
        dve.copy(sc1[:, 0:1], PS[0:48, 0:1])

        qstack = p.sb("qstack", [80, TOK], F32)
        kstack = p.sb("kstack", [80, TOK], F32)
        stg48 = p.sb("stg48", [48, TOK], BF16)
        stgb = p.sb("stgb", [80, TOK], BF16)
        stg4 = p.sb("stg4", [4, TOK], BF16)
        for (w, stack, do_scale, out_t) in ((wq, qstack, False, qh_out),
                                            (wk, kstack, True, kh_out)):
            dve.memset(stack[32:64, :], 0.0)
            pe.wait(dve)
            for n in range(NT):
                for k in range(KC):
                    pe.mm(PS[0:48, TOK + n * 512:TOK + (n + 1) * 512], w[:, k, :],
                          x_ln[:, k, n * 512:(n + 1) * 512], start=(k == 0), stop=(k == KC - 1))
            dve.wait(pe)
            dve.copy(stack[0:48, :], PS[0:48, TOK:2 * TOK])
            dve.ts(scr["xc"][0:48, 0, :], stack[0:48, :], dirs[0:48, 0:1], ALU.mult)
            dve.copy(stg48[0:48, :], scr["xc"][0:48, 0, 0:TOK])
            pe.wait(dve)
            for n in range(NT):
                pe.mm(PS[0:16, TOK + n * 512:TOK + (n + 1) * 512], t3[0:48, :],
                      stg48[0:48, n * 512:(n + 1) * 512])
            dve.wait(pe)
            dve.copy(stack[64:80, :], PS[0:16, TOK:2 * TOK])
            if do_scale:
                dve.ts(stack[0:48, :], stack[0:48, :], sc1[0:48, 0:1], ALU.mult)
            dve.copy(stgb[0:80, :], stack[0:80, :])
            for h in range(16):
                pe.wait(dve)
                for n in range(NT):
                    pe.mm(PS[0:4, TOK + n * 512:TOK + (n + 1) * 512],
                          selh[0:80, 4 * h:4 * h + 4], stgb[0:80, n * 512:(n + 1) * 512])
                dve.wait(pe)
                dve.wait(sp)
                dve.copy(stg4[0:4, :], PS[0:4, TOK:2 * TOK])
                sp.wait(dve)
                sp.dma(out_t[:, h, :], stg4[0:4, :])

        v_fm = p.sb("v_fm1", [128, KC, TOK], F32)
        pe.wait(dve)

        def evac_v(j, n, pap):
            dve.wait(pe)
            dve.copy(v_fm[:, j, n * 512:(n + 1) * 512], pap)
            return dve
        emit_proj(p, PS[:, 0:2 * TOK], wv,
                  lambda k, n: x_ln[:, k, n * 512:(n + 1) * 512], D, evac_v)
        vt = p.sb("vt_stage", [128, 1040], BF16)
        for t in range(16):
            dve.wait(sp)
            dve.memset(vt[:, :], 0.0)
            for c in range(KC):
                pe.wait(dve)
                pe.tr(PS[:, 3584:3712], v_fm[:, c, t * 128:(t + 1) * 128], p.ident[:, :])
                dve.wait(pe)
                dve.copy(vt[:, 65 * (2 * c) + 0:65 * (2 * c) + 64], PS[:, 3584:3648])
                dve.copy(vt[:, 65 * (2 * c + 1) + 0:65 * (2 * c + 1) + 64], PS[:, 3648:3712])
            for h in range(16):
                dve.memset(vt[:, 65 * h + 64:65 * h + 65], 1.0)
            sp.wait(dve)
            sp.dma(va_out[t * 128:(t + 1) * 128, :], vt[:, :])
        finish(p)

    return p.build(emit)


def build_attn_a1_main():
    p = Prog("attn_a1_main")
    S = SEQ

    def emit(p):
        pe, dve, act, pool, sp = p.pe, p.dve, p.act, p.pool, p.sp
        load_consts(p, ident_bf=True)
        x_in = p.dram("x_fm", [D, TOK], F32, kind="ExternalInput").ap()
        xr = x_in.rearrange("(c p) t -> c p t", p=128)
        qh_in = p.dram("qh", [4, 16, TOK], BF16, kind="ExternalInput").ap()
        kh_in = p.dram("khf", [4, 16, S], BF16, kind="ExternalInput").ap()
        va_in = p.dram("vaf", [S, 1040], BF16, kind="ExternalInput").ap()
        mask_in = p.dram("maskA", [S, TOK], BF16, kind="ExternalInput").ap()
        wo = load_w(p, "wo1", (D, D), (128, KC, D), (("(c p) h -> p c h",), dict(p=128)))

        vaug = p.sb("vaug", [128, 32, 1040], BF16)
        sp.dma(vaug[:, :, :], va_in.rearrange("(a p) d -> p a d", p=128))
        mask_sb = p.sb("maskS", [128, 32, 512], BF16)
        kh_h = p.sb("kh_h", [4, S], BF16)
        qh_h = p.sb("qh_h", [4, TOK], BF16)
        o_sb = p.sb("o_sb", [65, 512], F32)
        e_sb = p.sb("e_sb", [128, 512], BF16)
        zrow = p.sb("zrow", [1, 512], F32)
        attn_bf = p.sb("attn_bf", [128, KC, TOK], BF16)
        PS = p.ps("PS", [128, 4096])

        for q5 in range(NT):
            qsl = slice(q5 * 512, (q5 + 1) * 512)
            sp.wait(pe)
            sp.dma(mask_sb[:, :, :], mask_in.rearrange("(a p) t -> p a t", p=128)[:, :, qsl])
            for h in range(16):
                pool.wait(pe)
                pool.dma(kh_h[:, :], kh_in[:, h, :])
                pool.dma(qh_h[:, :], qh_in[:, h, :])
                pe.wait(pool); pe.wait(sp)
                psc = [PS[:, 0:512], PS[:, 512:1024]]
                pso = PS[0:65, 1024:1536]
                psz = PS[:, 1536:2048]
                for st in range(32):
                    pb = psc[st % 2]
                    if st >= 2:
                        pe.wait(act)
                    pe.mm(pb, kh_h[:, st * 128:(st + 1) * 128], qh_h[:, qsl],
                          start=True, stop=False)
                    pe.mm(pb, p.ident_bf[:, :], mask_sb[:, st, :],
                          start=False, stop=True)
                    act.wait(pe)
                    act.act(e_sb[:, :], pb, AF.Exp)
                    pe.wait(act)
                    pe.mm(pso, vaug[:, st, 65 * h:65 * (h + 1)],
                          e_sb[:, :], start=(st == 0), stop=(st == 31))
                act.wait(pe)
                act.wait(dve)
                act.act(o_sb[0:65, :], pso, AF.Identity)
                dve.wait(act)
                dve.recip(zrow[0:1, :], o_sb[64:65, :])
                pe.wait(dve)
                pe.mm(psz[0:64, :], p.onesf1[0:1, 0:64], zrow[0:1, :])
                dve.wait(pe)
                dve.tt(attn_bf[64 * (h % 2):64 * (h % 2) + 64, h // 2, qsl],
                       o_sb[0:64, :], psz[0:64, :], ALU.mult)
        x_out = p.dram("x_out", [D, TOK], F32, kind="ExternalOutput").ap()
        xo = x_out.rearrange("(c p) t -> c p t", p=128)
        xcb = p.sb("xcb", [128, 2, 512], F32)
        res = p.sb("res_d", [128, TOK], F32)
        pe.wait(dve)

        def evac_o(j, n, pap):
            b = n % 2
            sl = slice(n * 512, (n + 1) * 512)
            sp.wait(dve)
            sp.dma(xcb[:, b, :], xr[j, :, sl])
            dve.wait(sp)
            dve.wait(pe)
            dve.tt(res[:, sl], xcb[:, b, :], pap, ALU.add)
            if n == NT - 1:
                sp.wait(dve)
                sp.dma(xo[j, :, :], res[:, :])
            return dve
        emit_proj(p, PS[:, 0:2 * TOK], wo,
                  lambda k, n: attn_bf[:, k, n * 512:(n + 1) * 512], D, evac_o)
        finish(p)

    return p.build(emit)


def prep_attn_a1(inp, x_fms, qh=None, kh=None, va=None):
    g = lambda k: np.asarray(inp[k], dtype=np.float32)
    dirs = g("L1_attn_a_head_dirs")
    dirs = dirs / np.sqrt((dirs ** 2).sum(-1, keepdims=True))
    t3 = np.zeros((48, 16), dtype=np.float32)
    for h in range(16):
        t3[3 * h:3 * h + 3, h] = 1.0
    selh = np.zeros((80, 64), dtype=np.float32)
    for h in range(16):
        for j in range(3):
            selh[3 * h + j, 4 * h + j] = 1.0
        selh[64 + h, 4 * h + 3] = 1.0
    ones = np.ones((128, 128), dtype=np.float32)
    eye = np.eye(128, dtype=np.float32)
    pre_maps = []
    for c in range(NCORE):
        pre_maps.append(dict(x_fm=x_fms[c], wq48=g("L1_attn_a_wq"), wk48=g("L1_attn_a_wk"),
                             wv1=g("L1_attn_a_wv"), dirs48=dirs.reshape(48, 1),
                             t3mat=t3, selh=selh,
                             scale1=g("L1_attn_a_scale").reshape(1, 1),
                             ones_in=ones, eye_in=eye))
    if qh is None:
        return pre_maps
    main_maps = []
    for c in range(NCORE):
        b, hf = c // 2, c % 2
        khf = np.concatenate([kh[2 * b], kh[2 * b + 1]], axis=-1)
        vaf = np.concatenate([va[2 * b], va[2 * b + 1]], axis=0)
        q0 = hf * TOK
        sidx = np.arange(SEQ)[:, None]
        qidx = np.arange(q0, q0 + TOK)[None, :]
        mask = np.where(sidx <= qidx, 0.0, -1e9).astype(BF)
        main_maps.append(dict(x_fm=x_fms[c], qh=qh[c], khf=khf, vaf=vaf,
                              maskA=mask, wo1=g("L1_attn_a_wo"),
                              ones_in=ones, eye_in=eye))
    return main_maps



def build_attn_b1_pre():
    p = Prog("attn_b1_pre")

    def emit(p):
        pe, dve, act, pool, sp = p.pe, p.dve, p.act, p.pool, p.sp
        load_consts(p, ident=True)
        x_in = p.dram("x_fm", [D, TOK], F32, kind="ExternalInput").ap()
        wbuf = p.sb("wbufF", [128, KC, 2048], BF16)
        wq_in = p.dram("wqF", [D, 2048], F32, kind="ExternalInput").ap()
        wk_in = p.dram("wkF", [D, 2048], F32, kind="ExternalInput").ap()
        wv_in = p.dram("wvF", [D, D], F32, kind="ExternalInput").ap()
        bq = load_w(p, "bqF", (2048, 1), (128, 16, 1), (("(c p) o -> p c o",), dict(p=128)), dtype=F32)
        bk = load_w(p, "bkF", (2048, 1), (128, 16, 1), (("(c p) o -> p c o",), dict(p=128)), dtype=F32)
        bv = load_w(p, "bvF", (D, 1), (128, 8, 1), (("(c p) o -> p c o",), dict(p=128)), dtype=F32)
        cosf = load_w(p, "cosF", (128, TOK), (128, TOK), None, dtype=BF16)
        sinf = load_w(p, "sinF", (128, TOK), (128, TOK), None, dtype=BF16)
        q_out = p.dram("q_out", [2048, TOK], BF16, kind="ExternalOutput").ap()
        k_out = p.dram("k_out", [2048, TOK], BF16, kind="ExternalOutput").ap()
        va_out = p.dram("va_outF", [TOK, 1040], BF16, kind="ExternalOutput").ap()

        x_ln = p.sb("x_ln", [128, KC, TOK], BF16)
        scr = ln_scratch(p)
        PS = p.ps("PS", [128, 4096])
        psA = PS[0:1, 0:2 * TOK]
        psB = PS[:, 0:1024]

        dve.wait(sp); dve.wait(pool)
        dve.memset(scr["eps1"][0:1, 0:1], 1e-5)
        pe.wait(sp); pe.wait(pool); pe.wait(dve)
        emit_ln(p, x_in, x_ln, scr, psA, psB)

        qp = p.sb("qpF", [128, 16, TOK], BF16)
        tmpb = p.sb("tmpbF", [128, TOK], BF16)
        u1 = p.sb("u1F", [128, TOK], BF16)
        u2 = p.sb("u2F", [128, TOK], BF16)
        for (w_in, bias, out_t) in ((wq_in, bq, q_out), (wk_in, bk, k_out)):
            pool.wait(pe)
            pool.dma(wbuf[:, :, :], w_in.rearrange("(c p) o -> p c o", p=128))
            pe.wait(pool)

            def evac_q(j, n, pap, bias=bias):
                act.wait(pe)
                act.wait(dve)
                act.wait(sp)
                act.act(qp[:, j, n * 512:(n + 1) * 512], pap, AF.Identity,
                        bias=bias[:, j, 0:1])
                return act
            pe.wait(act); pe.wait(dve)
            emit_proj(p, PS[:, 0:2 * TOK], wbuf,
                      lambda k, n: x_ln[:, k, n * 512:(n + 1) * 512], 2048, evac_q)
            dve.wait(act)
            for c in [0, 1, 2, 3, 8, 9, 10, 11]:
                dve.copy(tmpb[:, :], qp[:, c, :])
                dve.tt(u1[:, :], tmpb[:, :], cosf[:, :], ALU.mult)
                dve.tt(u2[:, :], qp[:, c + 4, :], sinf[:, :], ALU.mult)
                dve.tt(qp[:, c, :], u1[:, :], u2[:, :], ALU.subtract)
                dve.tt(u1[:, :], tmpb[:, :], sinf[:, :], ALU.mult)
                dve.tt(u2[:, :], qp[:, c + 4, :], cosf[:, :], ALU.mult)
                dve.tt(qp[:, c + 4, :], u1[:, :], u2[:, :], ALU.add)
            sp.wait(dve)
            sp.wait(act)
            sp.dma(out_t.rearrange("(c p) t -> p c t", p=128), qp[:, :, :])

        v_fm = qp.ap().bitcast(F32).rearrange("p a t -> p (a t)").rearrange("p (c t) -> p c t", c=KC)
        wvv = wbuf[:, :, 0:1024]
        pool.wait(pe)
        pool.wait(p.sp)
        pool.dma(wvv, wv_in.rearrange("(c p) o -> p c o", p=128))
        pe.wait(pool)

        def evac_v(j, n, pap):
            act.wait(pe)
            act.wait(dve)
            act.wait(sp)
            act.act(v_fm[:, j, n * 512:(n + 1) * 512], pap, AF.Identity,
                    bias=bv[:, j, 0:1])
            return act
        pe.wait(act); pe.wait(dve)
        emit_proj(p, PS[:, 0:2 * TOK], wvv,
                  lambda k, n: x_ln[:, k, n * 512:(n + 1) * 512], D, evac_v)
        vt = p.sb("vt_stageF", [128, 1040], BF16)
        for t in range(16):
            dve.wait(sp)
            dve.wait(act)
            dve.memset(vt[:, :], 0.0)
            for c in range(KC):
                pe.wait(dve)
                pe.tr(PS[:, 3584:3712], v_fm[:, c, t * 128:(t + 1) * 128], p.ident[:, :])
                dve.wait(pe)
                dve.copy(vt[:, 65 * (2 * c) + 0:65 * (2 * c) + 64], PS[:, 3584:3648])
                dve.copy(vt[:, 65 * (2 * c + 1) + 0:65 * (2 * c + 1) + 64], PS[:, 3648:3712])
            for h in range(16):
                dve.memset(vt[:, 65 * h + 64:65 * h + 65], 1.0)
            sp.wait(dve)
            sp.dma(va_out[t * 128:(t + 1) * 128, :], vt[:, :])
        finish(p)

    return p.build(emit)


def build_attn_b1_main():
    p = Prog("attn_b1_main")
    S = SEQ
    LAM0 = 0.8 - 0.6 * math.exp(-0.3 * 0)

    def emit(p):
        pe, dve, act, pool, sp = p.pe, p.dve, p.act, p.pool, p.sp
        load_consts(p, ident=True, ident_bf=True)
        x_in = p.dram("x_fm", [D, TOK], F32, kind="ExternalInput").ap()
        xr = x_in.rearrange("(c p) t -> c p t", p=128)
        q_in = p.dram("qF", [2048, TOK], BF16, kind="ExternalInput").ap()
        k_in = p.dram("kFf", [2048, S], BF16, kind="ExternalInput").ap()
        va_in = p.dram("vaFf", [S, 1040], BF16, kind="ExternalInput").ap()
        mask_in = p.dram("maskF", [S, TOK], BF16, kind="ExternalInput").ap()
        wo = load_w(p, "woF", (D, D), (128, KC, D), (("(c p) h -> p c h",), dict(p=128)))
        bo = load_w(p, "boF", (D, 1), (128, 8, 1), (("(c p) o -> p c o",), dict(p=128)), dtype=F32)
        lamv = load_w(p, "lamin", (16, 256), (16, 256), None, dtype=F32)
        hnwf = load_w(p, "hnwF", (128, 8), (128, 8), None, dtype=F32)

        vaug = p.sb("vaugF", [128, 32, 1040], BF16)
        sp.dma(vaug[:, :, :], va_in.rearrange("(a p) d -> p a d", p=128))
        mask_sb = p.sb("maskSF", [128, 32, 512], BF16)
        ksl = p.sb("kslF", [64, 2, S], BF16)
        qsl_b = p.sb("qslF", [64, 2, TOK], BF16)
        o_f = p.sb("o_fF", [64, 3 * 512], F32)
        e_sb = p.sb("e_sbF", [128, 512], BF16)
        zrow = p.sb("zrowF", [1, 2 * 512], F32)
        attn_bf = p.sb("attn_bfF", [128, KC, TOK], BF16)
        lam_bc = p.sb("lam_bc", [128, 16], F32)
        sqb = p.sb("sqbF", [64, 512], BF16)
        PS = p.ps("PS", [128, 4096])

        lrow = p.sb("lrowF", [1, 16 * 2], F32)
        lr16 = p.sb("lr16F", [16, 2], F32)
        dve.wait(pool); dve.wait(sp)
        dve.tt(o_f[0:16, 0:64], lamv[0:16, 0:64], lamv[0:16, 64:128], ALU.mult)
        dve.red(lr16[0:16, 0:1], o_f[0:16, 0:64])
        dve.tt(o_f[0:16, 0:64], lamv[0:16, 128:192], lamv[0:16, 192:256], ALU.mult)
        dve.red(lr16[0:16, 1:2], o_f[0:16, 0:64])
        act.wait(dve)
        act.act(lr16[0:16, 0:2], lr16[0:16, 0:2], AF.Exp)
        dve.wait(act)
        dve.tt(lr16[0:16, 0:1], lr16[0:16, 0:1], lr16[0:16, 1:2], ALU.subtract)
        dve.ts(lr16[0:16, 0:1], lr16[0:16, 0:1], LAM0, ALU.add)
        pe.wait(dve)
        pe.tr(PS[0:1, 0:16], lr16[0:16, 0:1], p.ident[0:16, 0:16])
        dve.wait(pe)
        dve.copy(lrow[0:1, 0:16], PS[0:1, 0:16])
        pe.wait(dve)
        pe.mm(PS[:, 16:32], p.onesf1[0:1, :], lrow[0:1, 0:16])
        dve.wait(pe)
        dve.copy(lam_bc[:, :], PS[:, 16:32])
        pe.wait(dve)

        for q5 in range(NT):
            qsl = slice(q5 * 512, (q5 + 1) * 512)
            sp.wait(pe)
            sp.dma(mask_sb[:, :, :], mask_in.rearrange("(a p) t -> p a t", p=128)[:, :, qsl])
            for h in range(16):
                cA = h // 4
                rA = 32 * (h % 4)
                pool.wait(pe)
                for s_i in range(2):
                    off = 8 * s_i
                    pool.dma(ksl[0:32, s_i, :], k_in[(off + cA) * 128 + rA:(off + cA) * 128 + rA + 32, :])
                    pool.dma(ksl[32:64, s_i, :], k_in[(off + cA + 4) * 128 + rA:(off + cA + 4) * 128 + rA + 32, :])
                    pool.dma(qsl_b[0:32, s_i, :], q_in[(off + cA) * 128 + rA:(off + cA) * 128 + rA + 32, :])
                    pool.dma(qsl_b[32:64, s_i, :], q_in[(off + cA + 4) * 128 + rA:(off + cA + 4) * 128 + rA + 32, :])
                pe.wait(pool); pe.wait(sp)
                psc = [PS[:, 0:512], PS[:, 512:1024]]
                pso = [PS[0:65, 1024:1536], PS[0:65, 1536:2048]]
                for s_i in range(2):
                    for st in range(32):
                        pb = psc[st % 2]
                        if st >= 2 or s_i > 0:
                            pe.wait(act)
                        pe.mm(pb, ksl[:, s_i, st * 128:(st + 1) * 128], qsl_b[:, s_i, qsl],
                              start=True, stop=False)
                        pe.mm(pb, p.ident_bf[:, :], mask_sb[:, st, :],
                              start=False, stop=True)
                        act.wait(pe)
                        act.act(e_sb[:, :], pb, AF.Exp, scale=0.125)
                        pe.wait(act)
                        pe.mm(pso[s_i], vaug[:, st, 65 * h:65 * (h + 1)],
                              e_sb[:, :], start=(st == 0), stop=(st == 31))
                act.wait(pe); act.wait(dve)
                act.act(o_f[0:64, 0:512], pso[0][0:64, :], AF.Identity)
                act.act(zrow[0:1, 0:512], pso[0][64:65, :], AF.Identity)
                act.act(o_f[0:64, 512:1024], pso[1][0:64, :], AF.Identity)
                act.act(zrow[0:1, 512:1024], pso[1][64:65, :], AF.Identity)
                dve.wait(act)
                dve.recip(zrow[0:1, 0:1024], zrow[0:1, 0:1024])
                pe.wait(dve)
                pe.mm(PS[0:64, 2048:2560], p.onesf1[0:1, 0:64], zrow[0:1, 0:512])
                pe.mm(PS[0:64, 2560:3072], p.onesf1[0:1, 0:64], zrow[0:1, 512:1024])
                dve.wait(pe)
                dve.tt(o_f[0:64, 0:512], o_f[0:64, 0:512], PS[0:64, 2048:2560], ALU.mult)
                dve.tt(o_f[0:64, 512:1024], o_f[0:64, 512:1024], PS[0:64, 2560:3072], ALU.mult)
                dve.ts(o_f[0:64, 512:1024], o_f[0:64, 512:1024], lam_bc[0:64, h:h + 1],
                       ALU.mult)
                dve.tt(o_f[0:64, 0:512], o_f[0:64, 0:512], o_f[0:64, 512:1024], ALU.subtract)
                act.wait(dve)
                act.act(sqb[0:64, :], o_f[0:64, 0:512], AF.Square)
                pe.wait(act)
                pe.mm(PS[0:1, 3072:3584], p.ones_bf[0:64, 0:1], sqb[0:64, :])
                dve.wait(pe)
                dve.ts(zrow[0:1, 0:512], PS[0:1, 3072:3584], 1.0 / 64.0, ALU.mult, 1e-6, ALU.add)
                act.wait(dve)
                act.act(zrow[0:1, 0:512], zrow[0:1, 0:512], AF.Sqrt)
                dve.wait(act)
                dve.recip(zrow[0:1, 0:512], zrow[0:1, 0:512])
                pe.wait(dve)
                pe.mm(PS[0:64, 2048:2560], p.onesf1[0:1, 0:64], zrow[0:1, 0:512])
                dve.wait(pe)
                dve.tt(o_f[0:64, 1024:1536], o_f[0:64, 0:512], PS[0:64, 2048:2560], ALU.mult)
                dve.ts(attn_bf[64 * (h % 2):64 * (h % 2) + 64, h // 2, qsl],
                       o_f[0:64, 1024:1536], hnwf[64 * (h % 2):64 * (h % 2) + 64, h // 2:h // 2 + 1],
                       ALU.mult, (1.0 - LAM0), ALU.mult)
        x_out = p.dram("x_out", [D, TOK], F32, kind="ExternalOutput").ap()
        xo = x_out.rearrange("(c p) t -> c p t", p=128)
        xcb = p.sb("xcbF", [128, 2, 512], F32)
        res = p.sb("res_f", [128, TOK], F32)
        pe.wait(dve)

        def evac_o(j, n, pap):
            b = n % 2
            sl = slice(n * 512, (n + 1) * 512)
            act.wait(pe)
            act.wait(dve)
            act.wait(sp)
            act.act(res[:, sl], pap, AF.Identity, bias=bo[:, j, 0:1])
            sp.wait(dve)
            sp.dma(xcb[:, b, :], xr[j, :, sl])
            dve.wait(sp)
            dve.wait(act)
            dve.tt(res[:, sl], res[:, sl], xcb[:, b, :], ALU.add)
            if n == NT - 1:
                sp.wait(dve)
                sp.dma(xo[j, :, :], res[:, :])
            return dve
        emit_proj(p, PS[:, 0:2 * TOK], wo,
                  lambda k, n: attn_bf[:, k, n * 512:(n + 1) * 512], D, evac_o)
        finish(p)

    return p.build(emit)


def _permF():
    perm = np.zeros(2048, dtype=np.int64)
    offs = [0, 32, 64, 96]
    for n in range(2048):
        blk = n // 512
        h = (n % 512) // 32
        f = n % 32
        perm[n] = h * 128 + offs[blk] + f
    return perm


def prep_attn_b1(inp, x_fms, q=None, k=None, va=None):
    g = lambda kk: np.asarray(inp[kk], dtype=np.float32)
    perm = _permF()
    wq = np.ascontiguousarray(g("L1_attn_b_Wq").T[:, perm])
    wk = np.ascontiguousarray(g("L1_attn_b_Wk").T[:, perm])
    bq = np.ascontiguousarray(g("L1_attn_b_bq")[perm]).reshape(2048, 1)
    bk = np.ascontiguousarray(g("L1_attn_b_bk")[perm]).reshape(2048, 1)
    ones = np.ones((128, 128), dtype=np.float32)
    eye = np.eye(128, dtype=np.float32)
    half = 32
    inv = 1.0 / 10000.0 ** (np.arange(half) / half)
    tglob = np.arange(SEQ)
    cos_t = np.cos(tglob[None, :] * inv[:, None])
    sin_t = np.sin(tglob[None, :] * inv[:, None])
    cos_fm = np.tile(cos_t, (4, 1)).astype(np.float32)
    sin_fm = np.tile(sin_t, (4, 1)).astype(np.float32)
    pre_maps = []
    for c in range(NCORE):
        hf = c % 2
        csl = slice(hf * TOK, (hf + 1) * TOK)
        pre_maps.append(dict(x_fm=x_fms[c], wqF=wq, wkF=wk, bqF=bq, bkF=bk,
                             wvF=g("L1_attn_b_Wv").T.copy(), bvF=g("L1_attn_b_bv").reshape(D, 1),
                             cosF=np.ascontiguousarray(cos_fm[:, csl]),
                             sinF=np.ascontiguousarray(sin_fm[:, csl]),
                             ones_in=ones, eye_in=eye))
    if q is None:
        return pre_maps
    lam = np.concatenate([g("L1_attn_b_lq1"), g("L1_attn_b_lk1"),
                          g("L1_attn_b_lq2"), g("L1_attn_b_lk2")], axis=1)
    hnw = g("L1_attn_b_head_norm_w")
    hnwf = np.zeros((128, 8), dtype=np.float32)
    for h in range(16):
        hnwf[64 * (h % 2):64 * (h % 2) + 64, h // 2] = hnw[h]
    main_maps = []
    for c in range(NCORE):
        b, hf = c // 2, c % 2
        kf = np.concatenate([k[2 * b], k[2 * b + 1]], axis=-1)
        vaf = np.concatenate([va[2 * b], va[2 * b + 1]], axis=0)
        q0 = hf * TOK
        sidx = np.arange(SEQ)[:, None]
        qidx = np.arange(q0, q0 + TOK)[None, :]
        mask = np.where(sidx <= qidx, 0.0, -1e9).astype(BF)
        main_maps.append(dict(x_fm=x_fms[c], qF=q[c], kFf=kf, vaFf=vaf, maskF=mask,
                              woF=g("L1_attn_b_Wo").T.copy(), boF=g("L1_attn_b_bo").reshape(D, 1),
                              lamin=lam, hnwF=hnwf, ones_in=ones, eye_in=eye))
    return main_maps



def build_ffn1_pre():
    p = Prog("ffn1_pre")
    NG = 3072

    def emit(p):
        pe, dve, act, pool, sp = p.pe, p.dve, p.act, p.pool, p.sp
        load_consts(p, ident=True)
        x_in = p.dram("x_fm", [D, TOK], F32, kind="ExternalInput").ap()
        wih = load_w(p, "wih", (D, NG), (128, KC, NG), (("(c p) o -> p c o",), dict(p=128)))
        xw_out = p.dram("xw_out", [128, 24, TOK], BF16, kind="ExternalOutput").ap()
        mp_out = p.dram("mp_out", [128, KC], F32, kind="ExternalOutput").ap()

        x_ln = p.sb("x_ln", [128, KC, TOK], BF16)
        scr = ln_scratch(p)
        PS = p.ps("PS", [128, 4096])
        psA = PS[0:1, 0:2 * TOK]
        psB = PS[:, 0:1024]

        dve.wait(sp); dve.wait(pool)
        dve.memset(scr["eps1"][0:1, 0:1], 1e-5)
        pe.wait(sp); pe.wait(pool); pe.wait(dve)
        emit_ln(p, x_in, x_ln, scr, psA, psB)

        mp = p.sb("mp", [128, KC], F32)
        for c in range(KC):
            dve.red(mp[:, c:c + 1], x_ln[:, c, :])
        sp.wait(dve)
        sp.dma(mp_out, mp[:, :])

        xws = p.sb("xws_e1", [128, 24, 512], BF16)
        pe.wait(dve)
        for n in range(NT):
            nsl = slice(n * 512, (n + 1) * 512)
            for j in range(24):
                pj = PS[:, (j % 4) * 512:(j % 4) * 512 + 512]
                if j >= 4:
                    pe.wait(dve)
                for k in range(KC):
                    pe.mm(pj, wih[:, k, j * 128:(j + 1) * 128], x_ln[:, k, nsl],
                          start=(k == 0), stop=(k == KC - 1))
                dve.wait(pe)
                dve.wait(sp)
                dve.copy(xws[:, j, :], pj)
            sp.wait(dve)
            sp.dma(xw_out[:, :, nsl], xws[:, :, :])
        finish(p)

    return p.build(emit)


def build_ffn1_scan(T=SEQ):
    p = Prog("ffn1_scan")
    NG = 3072
    G = 1024
    KS = 36
    NTL = T // 512

    def emit(p):
        nc = p.nc
        pe, dve, act, pool, sp = p.pe, p.dve, p.act, p.pool, p.sp
        load_consts(p, ident=True)
        xw_in = p.dram("xw_t", [NTL, 128, 24, 512], BF16, kind="ExternalInput").ap()
        mps_in = p.dram("mps", [8, G], F32, kind="ExternalInput").ap()
        whh = load_w(p, "whhT", (G, NG), (128, KC, NG), (("(c p) o -> p c o",), dict(p=128)))
        cb4 = load_w(p, "cb4", (4, G), (4, G), None, dtype=F32)
        pairm = load_w(p, "pairm", (8, 4), (8, 4), None, dtype=F32)
        bmask = load_w(p, "bmask", (128, 32), (128, 32), None, dtype=F32)
        hs_out = p.dram("hs_out", [128, KC, T], BF16, kind="ExternalOutput").ap()

        PS = p.ps("PS", [128, 4096])
        H = p.sb("Hbuf", [128, KC, T + 1], BF16)
        xwb = p.sb("xwb", [128, 2, 24, 512], BF16)
        r_t = p.sb("r_t", [128, KC, 512], BF16)
        z_t = p.sb("z_t", [128, KC, 512], BF16)
        n_t = p.sb("n_t", [128, KC, 512], BF16)
        u12 = p.sb("u12", [128, 4, 512], F32)
        d1 = p.sb("d1", [128, 512], F32)
        da = p.sb("da", [128, 512], F32)

        ctx_v = xwb.ap().rearrange("p a b c -> p (a b c)")[:, 0:8192]
        ctx_v = ctx_v.rearrange("p (c g) -> p c g", c=KC)
        ctx_in = p.dram("ctxT", [G, G], F32, kind="ExternalInput").ap()
        pool.dma(ctx_v, ctx_in.rearrange("(c p) o -> p c o", p=128))

        mps = r_t.ap().bitcast(F32).rearrange("p a b -> p (a b)")[0:8, 0:G]
        mean4 = z_t.ap().bitcast(F32).rearrange("p a b -> p (a b)")[0:4, 0:G]
        hb = n_t.ap().bitcast(F32).rearrange("p a b -> p (a b)")[0:4, 0:G]
        mean_fm = p.sb("mean_fm", [128, KC, 4], BF16)
        u12f = u12.ap().rearrange("p a b -> p (a b)")
        hb_fm = u12f[:, 0:32]
        htmp0 = u12f[:, 32:64]
        sp.dma(mps[:, :], mps_in)
        pe.wait(sp)
        pe.wait(pool)
        for n2 in range(2):
            pe.mm(PS[0:4, n2 * 512:(n2 + 1) * 512], pairm[0:8, :],
                  mps[:, n2 * 512:(n2 + 1) * 512])
        dve.wait(pe)
        dve.ts(mean4[:, :], PS[0:4, 0:1024], 1.0 / SEQ, ALU.mult)
        pe.wait(dve)
        for c in range(KC):
            pe.tr(PS[:, 2048 + 4 * c:2048 + 4 * c + 4],
                  mean4[0:4, c * 128:(c + 1) * 128], p.ident[0:4, 0:4])
        act.wait(pe)
        act.act(mean_fm.ap().rearrange("p c b -> p (c b)"), PS[:, 2048:2080],
                AF.Identity)
        pe.wait(act)
        for n2 in range(2):
            for k in range(KC):
                pe.mm(PS[0:4, n2 * 512:(n2 + 1) * 512], mean_fm[:, k, :],
                      ctx_v[:, k, n2 * 512:(n2 + 1) * 512],
                      start=(k == 0), stop=(k == KC - 1))
        dve.wait(pe)
        dve.tt(hb[:, :], PS[0:4, 0:1024], cb4[0:4, :], ALU.add)
        pe.wait(dve)
        for c in range(KC):
            pe.tr(PS[:, 2048 + 4 * c:2048 + 4 * c + 4],
                  hb[0:4, 128 * c:128 * (c + 1)], p.ident[0:4, 0:4])
        dve.wait(pe)
        dve.copy(hb_fm[:, :], PS[:, 2048:2080])
        dve.tt(htmp0[:, :], hb_fm[:, :], bmask[:, :], ALU.mult)
        h0f = u12f[:, 64:72].rearrange("p (c b) -> p c b", b=1)
        dve.red(h0f, htmp0.rearrange("p (c b) -> p c b", c=KC))
        dve.copy(H[:, :, 0:1], h0f)
        dve.memset(H[:, :, 1:T + 1], 0.0)

        s_pec = Sem(p.ctx, nc, "s_pec")
        s_pet = Sem(p.ctx, nc, "s_pet")
        s_dp = Sem(p.ctx, nc, "s_dp")
        s_da = Sem(p.ctx, nc, "s_da")
        s_a = Sem(p.ctx, nc, "s_a")
        s_dt = Sem(p.ctx, nc, "s_dt")
        s_xw = Sem(p.ctx, nc, "s_xw")

        pe.eng.nop().then_inc(s_pet.h, 2)
        dve.eng.nop().then_inc(s_dp.h, 1)
        dve.eng.nop().then_inc(s_dt.h, 1)
        act.eng.nop().then_inc(s_a.h, 6)
        pe.wait(dve)

        ET_ = ET
        from concourse.bass import OrderedEngineSet as _OES

        def mkreg(name, et, val):
            r = nc.alloc_registers(name, engines=_OES([et]))
            nc.regs_mov(r, val)
            return r

        rP1 = mkreg("rP1", ET_.Pool, 0)
        rP2 = mkreg("rP2", ET_.Pool, 0)
        rX = mkreg("rX", ET_.PE, 0)
        rD = mkreg("rD", ET_.PE, 0)
        rP = mkreg("rP", ET_.PE, 0)
        rA = mkreg("rA", ET_.Activation, 0)
        rS1 = mkreg("rS1", ET_.DVE, 0)
        rPc = mkreg("rPc", ET_.DVE, 0)
        rS2 = mkreg("rS2", ET_.DVE, 4)
        rS3 = mkreg("rS3", ET_.DVE, 6)
        rPt = mkreg("rPt", ET_.DVE, 2)

        def inc(r, step=1):
            nc.regs_alu(r, r, step, op=ALU.add)

        with nc.Fori(0, KS, engines=[ET_.Pool]) as i:
            for t in range(NTL):
                inc(rP1)
                pool.eng.wait_ge(s_pet.h, rP1)
                pool.eng.wait_ge(s_dt.h, rP2)
                inc(rP2)
                pool.eng.dma_start(out=xwb[:, t % 2, :, :],
                                   in_=xw_in[t]).then_inc(s_xw.h, 16)

        with nc.Fori(0, KS, engines=[ET_.PE]) as i:
            for t in range(NTL):
                t0 = t * 512
                inc(rX, 16)
                inc(rD)
                pe.eng.wait_ge(s_xw.h, rX)
                pe.eng.wait_ge(s_dt.h, rD)
                for c in range(KC):
                    gq = t * 8 + c
                    q = (gq % 2) * 1536
                    pe.eng.wait_ge(s_dp.h, rP)
                    inc(rP)
                    for g3, base in ((0, c), (1, 8 + c), (2, 16 + c)):
                        ps = PS[:, q + g3 * 512: q + g3 * 512 + 512]
                        for kc in range(KC):
                            ins = pe.eng.matmul(
                                ps, whh[:, kc, base * 128:(base + 1) * 128],
                                H[:, kc, t0:t0 + 512],
                                start=(kc == 0), stop=(kc == KC - 1))
                    ins.then_inc(s_pec.h, 1)
                pe.eng.nop().then_inc(s_pet.h, 1)

        with nc.Fori(0, KS, engines=[ET_.Activation]) as i:
            for t in range(NTL):
                for c in range(KC):
                    gq = t * 8 + c
                    q2 = (gq % 2) * 2
                    u1 = u12[:, q2, :]
                    u2 = u12[:, q2 + 1, :]
                    inc(rA)
                    act.eng.wait_ge(s_da.h, rA)
                    act.eng.activation(out=r_t[:, c, :], in_=u1, func=AF.Tanh,
                                       scale=0.5).then_inc(s_a.h, 1)
                    act.eng.activation(out=z_t[:, c, :], in_=u2, func=AF.Tanh,
                                       scale=0.5).then_inc(s_a.h, 1)
                    inc(rA)
                    act.eng.wait_ge(s_da.h, rA)
                    act.eng.activation(out=n_t[:, c, :], in_=u1,
                                       func=AF.Tanh).then_inc(s_a.h, 1)

        with nc.Fori(0, KS, engines=[ET_.DVE]) as i:
            for t in range(NTL):
                t0 = t * 512
                for c in range(KC):
                    gq = t * 8 + c
                    q = (gq % 2) * 1536
                    q2 = (gq % 2) * 2
                    u1 = u12[:, q2, :]
                    u2 = u12[:, q2 + 1, :]
                    ps_r = PS[:, q:q + 512]
                    ps_z = PS[:, q + 512:q + 1024]
                    ps_h = PS[:, q + 1024:q + 1536]
                    xw_r = xwb[:, t % 2, c, :]
                    xw_z = xwb[:, t % 2, 8 + c, :]
                    xw_n = xwb[:, t % 2, 16 + c, :]
                    inc(rS1, 3)
                    dve.eng.wait_ge(s_a.h, rS1)
                    inc(rPc)
                    dve.eng.wait_ge(s_pec.h, rPc)
                    dve.eng.tensor_tensor(out=u1, in0=ps_r, in1=xw_r, op=ALU.add)
                    dve.eng.tensor_tensor(out=u2, in0=ps_z, in1=xw_z,
                                          op=ALU.add).then_inc(s_da.h, 1)
                    inc(rS2, 3)
                    dve.eng.wait_ge(s_a.h, rS2)
                    dve.eng.tensor_tensor(out=u1, in0=ps_h, in1=r_t[:, c, :],
                                          op=ALU.mult)
                    dve.eng.tensor_tensor(out=u1, in0=u1, in1=ps_h,
                                          op=ALU.add).then_inc(s_dp.h, 1)
                    dve.eng.tensor_scalar(out=u1, in0=u1, scalar1=0.5,
                                          scalar2=0.0, op0=ALU.mult)
                    dve.eng.tensor_tensor(out=u1, in0=u1, in1=xw_n,
                                          op=ALU.add).then_inc(s_da.h, 1)
                inc(rPt)
                dve.eng.wait_ge(s_pet.h, rPt)
                for c in range(KC):
                    inc(rS3, 3)
                    dve.eng.wait_ge(s_a.h, rS3)
                    dve.eng.tensor_tensor(out=d1[:, :], in0=H[:, c, t0:t0 + 512],
                                          in1=n_t[:, c, :], op=ALU.subtract)
                    dve.eng.tensor_tensor(out=da[:, :], in0=d1[:, :],
                                          in1=z_t[:, c, :], op=ALU.mult)
                    dve.eng.tensor_tensor(out=da[:, :], in0=da[:, :], in1=d1[:, :],
                                          op=ALU.add)
                    dve.eng.tensor_scalar(out=da[:, :], in0=da[:, :], scalar1=0.5,
                                          scalar2=0.0, op0=ALU.mult)
                    ins = dve.eng.tensor_tensor(out=H[:, c, t0 + 1:t0 + 513],
                                                in0=n_t[:, c, :], in1=da[:, :],
                                                op=ALU.add)
                ins.then_inc(s_dt.h, 1)

        sp.eng.wait_ge(s_dt.h, 8 * KS + 1)
        sp.dma(hs_out, H[:, :, 1:T + 1])
        finish(p)

    return p.build(emit)


def build_ffn1_main():
    p = Prog("ffn1_main")
    E8 = 8

    def emit(p):
        nc = p.nc
        pe, dve, act, pool, sp = p.pe, p.dve, p.act, p.pool, p.sp
        load_consts(p, ident=True)
        x_in = p.dram("x_fm", [D, TOK], F32, kind="ExternalInput").ap()
        xr = x_in.rearrange("(c p) t -> c p t", p=128)
        hs_in = p.dram("hs_my", [128, KC * TOK], BF16, kind="ExternalInput").ap()
        ew_in = p.dram("exprT", [D, D], F32, kind="ExternalInput").ap()
        gw_in = p.dram("gwT", [E8, D, 2048], F32, kind="ExternalInput").ap()
        uw_in = p.dram("uwT", [E8, D, 2048], F32, kind="ExternalInput").ap()
        dw_in = p.dram("dwT", [E8, 2048, D], F32, kind="ExternalInput").ap()
        sg_in = p.dram("sgT", [D, 2048], F32, kind="ExternalInput").ap()
        su_in = p.dram("suT", [D, 2048], F32, kind="ExternalInput").ap()
        sd_in = p.dram("sdT", [2048, D], F32, kind="ExternalInput").ap()

        x_ln = p.sb("x_ln", [128, KC, TOK], BF16)
        scr = ln_scratch(p)
        PS = p.ps("PS", [128, 4096])
        psA = PS[0:1, 0:2 * TOK]
        psB = PS[:, 0:1024]

        dve.wait(sp); dve.wait(pool)
        dve.memset(scr["eps1"][0:1, 0:1], 1e-5)
        pe.wait(sp); pe.wait(pool); pe.wait(dve)
        emit_ln(p, x_in, x_ln, scr, psA, psB)

        hs_bf = p.sb("hs_bf", [128, KC, TOK], BF16)
        pool.dma(hs_bf[:, :, :], hs_in.rearrange("p (c t) -> p c t", c=KC))
        rn_bf = p.sb("rn_bf", [128, KC, TOK], BF16)
        exw = p.sb("exw", [128, KC, 128], BF16)
        expr_e = p.sb("expr_e", [128, TOK], BF16)
        sqx = scr["sqc"]
        rowA = p.sb("rowA", [1, TOK], F32)
        rowB = p.sb("rowB", [1, TOK], F32)
        rowC = p.sb("rowC", [1, TOK], F32)
        sc_tm = p.sb("sc_tm", [128, 16, E8], F32)
        cw_tm = p.sb("cw_tm", [128, 16, E8], F32)

        ewr = ew_in.rearrange("(c p) o -> p c o", p=128)
        for e in range(E8):
            pool.wait(pe)
            pool.dma(exw[:, :, :], ewr[:, :, e * 128:(e + 1) * 128])
            act.wait(dve); act.wait(pe); act.wait(pool)
            act.act(sqx[:, :], hs_bf[:, e, :], AF.Square)
            pe.wait(act); pe.wait(pool); pe.wait(dve)
            for n in range(NT):
                pe.mm(psA[0:1, n * 512:(n + 1) * 512], p.ones_bf[:, 0:1],
                      sqx[:, n * 512:(n + 1) * 512])
            dve.wait(pe)
            dve.copy(rowA[0:1, :], psA[0:1, 0:TOK])
            pe.wait(dve)
            def evac_e(j, n, pap):
                act.wait(pe)
                act.act(expr_e[:, n * 512:(n + 1) * 512], pap, AF.Identity)
                return act
            pe.wait(act)
            emit_proj(p, PS[:, TOK:2 * TOK], exw,
                      lambda k, n: x_ln[:, k, n * 512:(n + 1) * 512], 128, evac_e,
                      banks=2)
            act.wait(pe)
            act.wait(dve)
            act.act(sqx[:, :], expr_e[:, :], AF.Square)
            pe.wait(act)
            for n in range(NT):
                pe.mm(psA[0:1, n * 512:(n + 1) * 512], p.ones_bf[:, 0:1],
                      sqx[:, n * 512:(n + 1) * 512])
            dve.wait(pe)
            dve.copy(rowB[0:1, :], psA[0:1, 0:TOK])
            pe.wait(dve)
            dve.wait(act)
            dve.tt(sqx[:, :], expr_e[:, :], hs_bf[:, e, :], ALU.mult)
            pe.wait(dve)
            for n in range(NT):
                pe.mm(psA[0:1, n * 512:(n + 1) * 512], p.ones_bf[:, 0:1],
                      sqx[:, n * 512:(n + 1) * 512])
            dve.wait(pe)
            dve.copy(rowC[0:1, :], psA[0:1, 0:TOK])
            act.wait(dve)
            act.act(rowA[0:1, :], rowA[0:1, :], AF.Sqrt)
            dve.wait(act)
            dve.ts(rowB[0:1, :], rowB[0:1, :], 0.0, ALU.max)
            act.wait(dve)
            act.act(rowB[0:1, :], rowB[0:1, :], AF.Sqrt)
            dve.wait(act)
            dve.copy(scr["stat"][0:1, 0:TOK], rowA[0:1, :])
            dve.ts(rowA[0:1, :], rowA[0:1, :], 1e-12, ALU.max)
            dve.recip(rowA[0:1, :], rowA[0:1, :])
            dve.tt(scr["stat"][0:1, 0:TOK], scr["stat"][0:1, 0:TOK], rowA[0:1, :], ALU.mult)
            dve.ts(scr["stat"][0:1, 0:TOK], scr["stat"][0:1, 0:TOK], 1e-8, ALU.max)
            dve.ts(rowB[0:1, :], rowB[0:1, :], 1e-8, ALU.max)
            dve.tt(rowB[0:1, :], rowB[0:1, :], scr["stat"][0:1, 0:TOK], ALU.mult)
            dve.recip(rowB[0:1, :], rowB[0:1, :])
            dve.tt(rowC[0:1, :], rowC[0:1, :], rowB[0:1, :], ALU.mult)
            dve.ts(rowC[0:1, :], rowC[0:1, :], -1.0, ALU.mult, 1.0, ALU.add)
            for t in range(16):
                pe.wait(dve)
                pe.tr(PS[:, 3072 + t:3072 + t + 1] if False else PS[:, 3072 + t:3073 + t],
                      rowC[0:1, t * 128:(t + 1) * 128], p.ident[0:1, 0:1])
            dve.wait(pe)
            dve.copy(sc_tm.ap().rearrange("p a b -> p a b")[:, :, e],
                     PS[:, 3072:3088])

        ctb = p.sb("ctb", [128, 16 * E8], F32)
        dve.ts(sc_tm.ap().rearrange("p a b -> p (a b)"),
               sc_tm.ap().rearrange("p a b -> p (a b)"), 9.0, ALU.mult)

        for t in range(16):
            s_t = sc_tm[:, t, :]
            dve.red(ctb[:, 0:1], s_t, op=ALU.max)
            dve.ts(ctb[:, 8:16], s_t, ctb[:, 0:1], ALU.is_ge)
            dve.ts(ctb[:, 16:24], ctb[:, 8:16], -1e30, ALU.mult)
            dve.tt(ctb[:, 24:32], s_t, ctb[:, 16:24], ALU.add)
            dve.red(ctb[:, 1:2], ctb[:, 24:32], op=ALU.max)
            dve.ts(ctb[:, 32:40], ctb[:, 24:32], ctb[:, 1:2], ALU.is_ge)
            dve.tt(ctb[:, 2:3], ctb[:, 0:1], ctb[:, 1:2], ALU.subtract)
            act.wait(dve)
            act.act(ctb[:, 3:4], ctb[:, 2:3], AF.Sigmoid)
            dve.wait(act)
            dve.ts(ctb[:, 4:5], ctb[:, 3:4], -1.0, ALU.mult, 1.0, ALU.add)
            dve.ts(ctb[:, 8:16], ctb[:, 8:16], ctb[:, 3:4], ALU.mult)
            dve.ts(ctb[:, 32:40], ctb[:, 32:40], ctb[:, 4:5], ALU.mult)
            dve.tt(cw_tm[:, t, :], ctb[:, 8:16], ctb[:, 32:40], ALU.add)

        cwrow = p.sb("cwrow", [1, TOK], F32)
        cw_bc = p.sb("cw_bc", [128, 512], F32)
        rnflat = rn_bf.ap().rearrange("p c t -> p (c t)").rearrange("p (a b) -> p a b", a=32)
        gbuf = rnflat[:, 0:16, :]
        fbuf = rnflat[:, 16:32, :]
        wmo = hs_bf.ap().rearrange("p c t -> p (c t)")
        w8v = wmo.rearrange("p (c o) -> p c o", c=KC)
        w16v = wmo.rearrange("p (c o) -> p c o", c=16)
        accn = p.sb("accn", [128, KC, 512], F32)
        x_out = p.dram("x_out", [D, TOK], F32, kind="ExternalOutput").ap()
        xo = x_out.rearrange("(c p) t -> c p t", p=128)
        xtm_out = p.dram("xtm_out", [TOK, D], F32, kind="ExternalOutput").ap()
        res = p.sb("res_e3", [128, 512], F32)

        for n in range(NT):
            nsl = slice(n * 512, (n + 1) * 512)
            dve.wait(sp)
            first = True
            for e in range(E8 + 1):
                shared = (e == E8)
                if not shared:
                    pe.wait(dve)
                    for t4 in range(4):
                        tt_ = n * 4 + t4
                        pe.tr(PS[0:1, 3072 + t4 * 128:3072 + (t4 + 1) * 128],
                              cw_tm[:, tt_, e:e + 1], p.ident[:, :])
                    dve.wait(pe)
                    dve.copy(cwrow[0:1, 0:512], PS[0:1, 3072:3584])
                    pe.wait(dve)
                    pe.mm(PS[:, 3584:4096], p.onesf1[0:1, :], cwrow[0:1, 0:512])
                    dve.wait(pe)
                    dve.copy(cw_bc[:, :], PS[:, 3584:4096])
                pool.wait(pe)
                pool.dma(w8v[:, :, :], (sg_in if shared else gw_in[e]).rearrange("(c p) o -> p c o", p=128))
                pe.wait(pool); pe.wait(act); pe.wait(dve)
                for j in range(16):
                    pj = PS[:, (j % 2) * 512:(j % 2) * 512 + 512]
                    if j >= 2:
                        pe.wait(act)
                    for k in range(KC):
                        pe.mm(pj, w8v[:, k, j * 128:(j + 1) * 128], x_ln[:, k, nsl],
                              start=(k == 0), stop=(k == KC - 1))
                    act.wait(pe)
                    act.act(gbuf[:, j, :], pj, AF.Silu)
                pool.wait(pe)
                pool.dma(w8v[:, :, :], (su_in if shared else uw_in[e]).rearrange("(c p) o -> p c o", p=128))
                pe.wait(pool); pe.wait(act)
                for j in range(16):
                    pj = PS[:, (j % 2) * 512:(j % 2) * 512 + 512]
                    if j >= 2:
                        pe.wait(dve)
                    for k in range(KC):
                        pe.mm(pj, w8v[:, k, j * 128:(j + 1) * 128], x_ln[:, k, nsl],
                              start=(k == 0), stop=(k == KC - 1))
                    dve.wait(pe)
                    dve.wait(act)
                    dve.tt(fbuf[:, j, :], gbuf[:, j, :], pj, ALU.mult)
                pool.wait(pe)
                pool.dma(w16v[:, :, :], (sd_in if shared else dw_in[e]).rearrange("(c p) o -> p c o", p=128))
                pe.wait(pool); pe.wait(dve)
                for j in range(KC):
                    pj = PS[:, 2048 + j % 2 * 512:2048 + j % 2 * 512 + 512]
                    if j >= 2:
                        pe.wait(dve)
                    for k in range(16):
                        pe.mm(pj, w16v[:, k, j * 128:(j + 1) * 128], fbuf[:, k, :],
                              start=(k == 0), stop=(k == 15))
                    dve.wait(pe)
                    if shared:
                        dve.tt(accn[:, j, :], accn[:, j, :], pj, ALU.add)
                    else:
                        dve.tt(res[:, :], pj, cw_bc[:, :], ALU.mult)
                        if first:
                            dve.copy(accn[:, j, :], res[:, :])
                        else:
                            dve.tt(accn[:, j, :], accn[:, j, :], res[:, :], ALU.add)
                first = False
            for c in range(KC):
                b = c % 2
                sp.wait(dve)
                sp.dma(scr["xc"][:, b, 0:512], xr[c, :, nsl])
                dve.wait(sp)
                dve.tt(res[:, :], accn[:, c, :], scr["xc"][:, b, 0:512], ALU.add)
                sp.wait(dve)
                sp.dma(xo[c, :, nsl], res[:, :])
        finish(p)

    return p.build(emit)



def _rep_matrix():
    rep = np.zeros((KC, 16, 128), dtype=np.float32)
    for c in range(KC):
        for pp in range(128):
            rep[c, (128 * c + pp) // 64, pp] = 1.0
    return rep


def _shard_tokens(x):
    out = []
    for c in range(NCORE):
        b, h = c // 2, c % 2
        out.append(np.ascontiguousarray(x[b, h * TOK:(h + 1) * TOK, :]))
    return out


def _fm(xtm):
    return np.ascontiguousarray(xtm.T)


def prep_attn_a0_pre(inp, x_fms):
    W_Q = np.asarray(inp["L0_attn_a_W_Q"], dtype=np.float32)
    W_K = np.asarray(inp["L0_attn_a_W_K"], dtype=np.float32)
    W_V = np.asarray(inp["L0_attn_a_W_V"], dtype=np.float32)
    wq16 = np.ascontiguousarray(W_Q.transpose(1, 0, 2).reshape(D, 16))
    wk16 = np.ascontiguousarray(W_K.transpose(1, 0, 2).reshape(D, 16))
    wv = np.ascontiguousarray(W_V.transpose(1, 0, 2).reshape(D, D))
    rep = _rep_matrix()
    ones = np.ones((128, 128), dtype=np.float32)
    return [dict(x_fm=x_fms[c], wq16=wq16, wk16=wk16, wv=wv, rep=rep,
                 ones_in=ones) for c in range(NCORE)]


def prep_attn_a0_main(inp, x_fms, trig, v_, tot):
    W_O = np.asarray(inp["L0_attn_a_W_O"], dtype=np.float32)
    wo2 = np.ascontiguousarray(W_O.transpose(2, 1, 0).reshape(D, D))
    rep = _rep_matrix()
    ones = np.ones((128, 128), dtype=np.float32)
    maps = []
    for c in range(NCORE):
        if c % 2 == 0:
            offs = np.zeros((128, 20), dtype=np.float32)
        else:
            offs = np.asarray(tot[c - 1], dtype=np.float32)
        maps.append(dict(x_fm=x_fms[c], trig=np.asarray(trig[c]),
                         v_in=np.asarray(v_[c]), offs_in=offs, wo2=wo2,
                         rep=rep, ones_in=ones))
    return maps


def prep_ffn1_pre(inp, x_fms):
    g = lambda k: np.asarray(inp[k], dtype=np.float32)
    ones = np.ones((128, 128), dtype=np.float32)
    eye = np.eye(128, dtype=np.float32)
    wih = np.ascontiguousarray(g("L1_ffn_gru_w_ih").T)
    return [dict(x_fm=x_fms[c], wih=wih, ones_in=ones, eye_in=eye)
            for c in range(NCORE)]


def prep_ffn1_scan(inp, xw, mp):
    g = lambda k: np.asarray(inp[k], dtype=np.float32)
    ones = np.ones((128, 128), dtype=np.float32)
    eye = np.eye(128, dtype=np.float32)
    mps = np.stack([np.asarray(m).T.reshape(D) for m in mp], axis=0).astype(np.float32)
    pairm = np.zeros((8, 4), dtype=np.float32)
    for r in range(8):
        pairm[r, r // 2] = 1.0
    whhT = np.ascontiguousarray(g("L1_ffn_gru_w_hh").T)
    ctxT = np.ascontiguousarray(g("L1_ffn_ctx_w").T)
    cb4 = np.tile(g("L1_ffn_ctx_b")[None, :], (4, 1))
    xw_ts = []
    for b in range(4):
        xw_b = np.concatenate([np.asarray(xw[2 * b]), np.asarray(xw[2 * b + 1])],
                              axis=-1)
        xw_ts.append(np.ascontiguousarray(
            xw_b.reshape(128, 24, SEQ // 512, 512).transpose(2, 0, 1, 3)))
    maps = []
    for c in range(NCORE):
        b = c // 2
        bm2 = np.zeros((128, 8, 4), dtype=np.float32)
        bm2[:, :, b] = 1.0
        maps.append(dict(xw_t=xw_ts[b], mps=mps, whhT=whhT,
                         ctxT=ctxT, cb4=cb4, pairm=pairm,
                         bmask=bm2.reshape(128, 32), ones_in=ones, eye_in=eye))
    return maps


def prep_ffn1_main(inp, x_fms, hs):
    g = lambda k: np.asarray(inp[k], dtype=np.float32)
    ones = np.ones((128, 128), dtype=np.float32)
    eye = np.eye(128, dtype=np.float32)
    exprT = np.ascontiguousarray(g("L1_ffn_expr_w").T)
    gwT = np.ascontiguousarray(g("L1_ffn_gate_w").transpose(0, 2, 1))
    uwT = np.ascontiguousarray(g("L1_ffn_up_w").transpose(0, 2, 1))
    dwT = np.ascontiguousarray(g("L1_ffn_down_w").transpose(0, 2, 1))
    sgT = np.ascontiguousarray(g("L1_ffn_shared_gate_w").T)
    suT = np.ascontiguousarray(g("L1_ffn_shared_up_w").T)
    sdT = np.ascontiguousarray(g("L1_ffn_shared_down_w").T)
    maps = []
    for c in range(NCORE):
        hf = c % 2
        full = np.asarray(hs[c])
        hs_my = np.ascontiguousarray(
            full[:, :, hf * TOK:(hf + 1) * TOK]).reshape(128, KC * TOK)
        maps.append(dict(x_fm=x_fms[c], hs_my=hs_my, exprT=exprT, gwT=gwT,
                         uwT=uwT, dwT=dwT, sgT=sgT, suT=suT, sdT=sdT,
                         ones_in=ones, eye_in=eye))
    return maps


_BUILDERS = {
    "A1": lambda: build_attn_a0_pre(),
    "A2": lambda: build_attn_a0_main(),
    "B1": lambda: build_ffn0_pre(),
    "B2": lambda: build_ffn0_main(),
    "C": lambda: build_attn_b0(),
    "D1": lambda: build_attn_a1_pre(),
    "D2": lambda: build_attn_a1_main(),
    "E1": lambda: build_ffn1_pre(),
    "E2": lambda: build_ffn1_scan(SEQ),
    "E3": lambda: build_ffn1_main(),
    "F1": lambda: build_attn_b1_pre(),
    "F2": lambda: build_attn_b1_main(),
}


class _LazyProgs:
    def __getitem__(self, tag):
        if tag not in _PROGRAMS:
            _PROGRAMS[tag] = _BUILDERS[tag]()
        return _PROGRAMS[tag]


def _progs():
    return _LazyProgs()


def kernel(**inputs):
    from concourse.bass_utils import run_bass_kernel_spmd
    inp = inputs
    P = _progs()
    cids = list(range(NCORE))

    def run(tag, maps, outs):
        res = run_bass_kernel_spmd(P[tag].nc, maps, core_ids=cids)
        return [[res.results[c][o] for c in cids] for o in outs]

    x0 = np.asarray(inputs["L0_attn_a_query"], dtype=np.float32)
    x_fms = [_fm(s) for s in _shard_tokens(x0)]
    trig, v_, tot = run("A1", prep_attn_a0_pre(inp, x_fms),
                        ["trig_out", "v_out", "tot_out"])
    (x_fms,) = run("A2", prep_attn_a0_main(inp, x_fms, trig, v_, tot), ["x_out"])

    def ffn0(x_fms):
        xl, ss = run("B1", prep_ffn0_pre(inp, x_fms), ["xln_out", "s_out"])
        (x,) = run("B2", prep_ffn0_main(inp, x_fms, xl, ss), ["x_out"])
        return x

    x_fms = ffn0(x_fms)
    (x_fms,) = run("C", prep_attn_b0(inp, x_fms), ["x_out"])
    x_fms = ffn0(x_fms)
    qh, kh, va = run("D1", prep_attn_a1(inp, x_fms), ["qh_out", "kh_out", "va_out"])
    (x_fms,) = run("D2", prep_attn_a1(inp, x_fms, qh, kh, va), ["x_out"])

    def ffn1(x_fms):
        xw, mp = run("E1", prep_ffn1_pre(inp, x_fms), ["xw_out", "mp_out"])
        (hs,) = run("E2", prep_ffn1_scan(inp, xw, mp), ["hs_out"])
        (x,) = run("E3", prep_ffn1_main(inp, x_fms, hs), ["x_out"])
        return x

    x_fms = ffn1(x_fms)
    q, k_, va2 = run("F1", prep_attn_b1(inp, x_fms), ["q_out", "k_out", "va_outF"])
    (x_fms,) = run("F2", prep_attn_b1(inp, x_fms, q, k_, va2), ["x_out"])
    x_fms = ffn1(x_fms)

    out = np.zeros((4, SEQ, D), dtype=np.float32)
    for c in range(NCORE):
        b, h = c // 2, c % 2
        out[b, h * TOK:(h + 1) * TOK, :] = np.asarray(x_fms[c], dtype=np.float32).T
    return out
